# version 04
# baseline (speedup 1.0000x reference)
import ml_dtypes
"""Raw-Bass builder with automatic semaphore dependency tracking.

Records ops per engine with read/write sets (buffer = tensor name), computes
required semaphore waits (vector-clock style, transitive), and emits the
program with explicit wait_ge / then_inc on per-engine semaphores.

Engines: sync, vector, scalar, tensor, gpsimd. Each engine has a compute
semaphore C_<eng> (+1 per op) and a DMA semaphore D_<eng> (+16 per dma).
Collectives use sem CC (+1 each), issued on gpsimd.
"""
import numpy as np
from contextlib import ExitStack
import concourse.bass as bass
import concourse.mybir as mybir

F32 = mybir.dt.float32
BF16 = mybir.dt.bfloat16
F8E5 = mybir.dt.float8e5
I32 = mybir.dt.int32
AF = mybir.ActivationFunctionType
ALU = mybir.AluOpType
AX = mybir.AxisListType

ENGINES = ["sync", "vector", "scalar", "tensor", "gpsimd"]


def _names(aps):
    out = []
    for ap in aps:
        if ap is None:
            continue
        t = getattr(ap, "tensor", None)
        n = getattr(t, "name", None)
        if n is None:
            raise ValueError(f"cannot extract tensor name from {ap}")
        out.append(n)
    return out


class Prog:
    def __init__(self, name="prog"):
        self.nc = bass.Bass("TRN2", target_bir_lowering=False)
        self.es = ExitStack()
        self.recs = {e: [] for e in ENGINES}
        self.sem_names = [f"C_{e}" for e in ENGINES] + [f"D_{e}" for e in ENGINES] + ["CC"]
        self.cnt = {s: 0 for s in self.sem_names}
        self.clock = {e: {} for e in ENGINES}
        self.last_write = {}
        self.readers = {}
        self.dma_gate = {}
        self._finalized = False
        self._n_ops = 0

    def din(self, name, shape, dtype):
        return self.nc.dram_tensor(name, list(shape), dtype, kind="ExternalInput").ap()

    def dout(self, name, shape, dtype):
        return self.nc.dram_tensor(name, list(shape), dtype, kind="ExternalOutput").ap()

    def dint(self, name, shape, dtype):
        return self.nc.dram_tensor(name, list(shape), dtype, kind="Internal").ap()

    def sb(self, name, shape, dtype):
        return self.es.enter_context(self.nc.sbuf_tensor(name, list(shape), dtype))

    def ps(self, name, shape, dtype=F32):
        return self.es.enter_context(self.nc.psum_tensor(name, list(shape), dtype))

    def _need(self, eng, reads, writes):
        need = {}
        def add(sem, val, pclock):
            cur = need.get(sem)
            if cur is None or val > cur[0]:
                need[sem] = (val, pclock)
        for r in reads:
            lw = self.last_write.get(r)
            if lw is not None:
                add(lw[0], lw[1], lw[2])
        for w in writes:
            lw = self.last_write.get(w)
            if lw is not None:
                add(lw[0], lw[1], lw[2])
            for sem, (val, rclock) in self.readers.get(w, {}).items():
                add(sem, val, rclock)
        clk = self.clock[eng]
        waits = {}
        for sem, (val, pclock) in need.items():
            if clk.get(sem, 0) >= val:
                continue
            waits[sem] = val
            clk[sem] = val
            if pclock:
                for s2, v2 in pclock.items():
                    if clk.get(s2, 0) < v2:
                        clk[s2] = v2
        return waits

    def op(self, eng, emit, reads=(), writes=(), dma=False, collective=False, tag=""):
        rnames = [r if isinstance(r, str) else _names([r])[0] for r in reads if r is not None]
        wnames = [w if isinstance(w, str) else _names([w])[0] for w in writes if w is not None]
        waits = self._need(eng, rnames, wnames)
        sem = "CC" if collective else (f"D_{eng}" if dma else f"C_{eng}")
        amt = 1 if collective else (16 if dma else 1)
        unordered = sem.startswith("D_") or sem == "CC"
        if unordered:
            gate = self.dma_gate.get(sem)
            if gate:
                clk = self.clock[eng]
                extra = {}
                for s2, v2 in gate.items():
                    if s2 != sem and clk.get(s2, 0) < v2:
                        extra[s2] = v2
                        clk[s2] = v2
                for s2, v2 in extra.items():
                    waits[s2] = max(waits.get(s2, 0), v2)
        for s2 in list(waits.keys()):
            if s2.startswith("D_") or s2 == "CC":
                waits[s2] = self.cnt[s2]
                self.clock[eng][s2] = self.cnt[s2]
        self.cnt[sem] += amt
        cafter = self.cnt[sem]
        for s2 in waits:
            if s2.startswith("D_") or s2 == "CC":
                g = self.dma_gate.setdefault(s2, {})
                g[sem] = max(g.get(sem, 0), cafter)
        myclock = dict(self.clock[eng])
        myclock[sem] = max(myclock.get(sem, 0), cafter)
        for w in wnames:
            self.last_write[w] = (sem, cafter, myclock)
            self.readers[w] = {}
        for r in rnames:
            d = self.readers.setdefault(r, {})
            old = d.get(sem)
            if old is None or cafter > old[0]:
                d[sem] = (cafter, myclock)
        self.recs[eng].append((waits, emit, sem, amt, tag))
        self._n_ops += 1
        return cafter

    def constf(self, value, dtype=F32):
        key = ("constf", float(value), dtype)
        cache = self.__dict__.setdefault("_consts", {})
        if key not in cache:
            tile = self.__dict__.setdefault("_const_tile_" + str(dtype), None)
            name = f"cst_{len(cache)}"
            t = self.sb(name, [128, 1], dtype)
            self.memset("vector", t[:], float(value), tag="const")
            cache[key] = t
        return cache[key][:]

    def bcrow(self, which):
        cache = self.__dict__.setdefault("_consts", {})
        key = ("bcrow", which)
        if key not in cache:
            t = self.sb(f"cbc_{which}", [2, 128], F32)
            self.memset("vector", t[:], 0.0, tag="const")
            row = 0 if which == "r0" else 1
            self.memset("vector", t[row:row + 1, :], 1.0, tag="const")
            cache[key] = t
        return cache[key][:]

    def ones2(self, dtype=BF16):
        cache = self.__dict__.setdefault("_consts", {})
        key = ("ones2", dtype)
        if key not in cache:
            t = self.sb(f"cones2_{str(dtype)[-4:]}", [128, 2], dtype)
            self.memset("vector", t[:], 1.0, tag="const")
            cache[key] = t
        return cache[key][:]

    def sel16(self):
        cache = self.__dict__.setdefault("_consts", {})
        if "sel16" not in cache:
            a = self.din("c_sel16", [16, 2048], F32)
            t = self.sb("s_c_sel16", [16, 2048], F32)
            self.dma("sync", t[:], a)
            cache["sel16"] = t
        return cache["sel16"][:]

    def iden(self):
        cache = self.__dict__.setdefault("_consts", {})
        if "iden" not in cache:
            a = self.din("c_iden", [128, 128], BF16)
            t = self.sb("s_c_iden", [128, 128], BF16)
            self.dma("sync", t[:], a)
            cache["iden"] = t
        return cache["iden"][:]

    def bcast_row(self, psum_slice, row_pair_ap, tag="bcast"):
        return self.mm(psum_slice, [(self.bcrow("r0"), row_pair_ap)], tag=tag)

    def dma(self, eng, out, in_, **kw):
        return self.op(eng, lambda e: e.dma_start(out=out, in_=in_, **kw),
                       reads=[in_], writes=[out], dma=True, tag="dma")

    def cc(self, kind, alu, groups, ins, outs, tag="cc"):
        return self.op("gpsimd",
                       lambda e: e.collective_compute(kind, alu, replica_groups=groups,
                                                      ins=[a.opt() for a in ins],
                                                      outs=[a.opt() for a in outs]),
                       reads=list(ins), writes=list(outs), collective=True, tag=tag)

    def mm(self, psum, pairs, reads=None, tag="mm"):
        n = len(pairs)
        def emit(e):
            last = None
            for i, (l, r) in enumerate(pairs):
                last = e.matmul(psum, l, r, start=(i == 0), stop=(i == n - 1))
            return last
        rd = reads if reads is not None else [l for l, r in pairs] + [r for l, r in pairs]
        return self.op("tensor", emit, reads=rd, writes=[psum], tag=tag)

    def mm_acc(self, psum, pairs, start, stop, tag="mm_acc"):
        n = len(pairs)
        def emit(e):
            last = None
            for i, (l, r) in enumerate(pairs):
                last = e.matmul(psum, l, r, start=(start and i == 0), stop=(stop and i == n - 1))
            return last
        rd = [l for l, r in pairs] + [r for l, r in pairs] + ([psum] if not start else [])
        return self.op("tensor", emit, reads=rd, writes=[psum], tag=tag)

    def mmflex(self, emitlist, reads, writes, tag="mmflex"):
        def emit(e):
            last = None
            for f in emitlist:
                last = f(e)
            return last
        return self.op("tensor", emit, reads=reads, writes=writes, tag=tag)

    def act(self, out, in_, func, scale=1.0, bias=0.0, accum_out=None, tag="act"):
        reads = [in_]
        if not isinstance(scale, (int, float)):
            reads.append(scale)
        if not isinstance(bias, (int, float)):
            reads.append(bias)
        kw = {}
        if accum_out is not None:
            kw["accum_out"] = accum_out
        return self.op("scalar",
                       lambda e: e.activation(out, in_, func, bias=bias, scale=scale, **kw),
                       reads=reads, writes=[out] + ([accum_out] if accum_out is not None else []),
                       tag=tag)

    def tt(self, eng, out, a, b, alu, tag="tt"):
        return self.op(eng, lambda e: e.tensor_tensor(out, a, b, op=alu),
                       reads=[a, b], writes=[out], tag=tag)

    def ts(self, eng, out, in_, s1, s2=None, op0=ALU.mult, op1=ALU.bypass, tag="ts"):
        reads = [in_] + [s for s in (s1, s2) if not isinstance(s, (int, float)) and s is not None]
        def emit(e):
            return e.tensor_scalar(out, in_, s1, s2, op0=op0, op1=op1)
        return self.op(eng, emit, reads=reads, writes=[out], tag=tag)

    def stt(self, eng, out, in0, scalar, in1, op0, op1, tag="stt"):
        reads = [in0, in1] + ([scalar] if not isinstance(scalar, (int, float)) else [])
        return self.op(eng, lambda e: e.scalar_tensor_tensor(out, in0, scalar, in1, op0=op0, op1=op1),
                       reads=reads, writes=[out], tag=tag)

    def copy(self, eng, out, in_, tag="copy"):
        return self.op(eng, lambda e: e.tensor_copy(out, in_), reads=[in_], writes=[out], tag=tag)

    def scopy(self, out, in_, tag="scopy"):
        return self.op("scalar", lambda e: e.copy(out, in_), reads=[in_], writes=[out], tag=tag)

    def red(self, eng, out, in_, alu, axis=AX.X, absval=None, tag="red"):
        def emit(e):
            return e.tensor_reduce(out, in_, axis=axis, op=alu, apply_absolute_value=absval)
        return self.op(eng, emit, reads=[in_], writes=[out], tag=tag)

    def scan(self, eng, out, a, b, initial, tag="scan"):
        reads = [a, b] + ([initial] if not isinstance(initial, (int, float)) else [])
        return self.op(eng, lambda e: e.tensor_tensor_scan(out, a, b, initial, op0=ALU.mult, op1=ALU.add),
                       reads=reads, writes=[out], tag=tag)

    def memset(self, eng, ap, val, tag="memset"):
        return self.op(eng, lambda e: e.memset(ap, val), writes=[ap], tag=tag)

    def transpose(self, psum_out, in_, identity, tag="tr"):
        return self.op("tensor", lambda e: e.transpose(psum_out, in_, identity),
                       reads=[in_, identity], writes=[psum_out], tag=tag)

    def finalize(self):
        assert not self._finalized
        self._finalized = True
        sems = {}
        for s in self.sem_names:
            if self.cnt[s] > 0 or s.startswith("C_"):
                sems[s] = self.es.enter_context(self.nc.semaphore(s))
        self._semh = sems
        block = self.es.enter_context(self.nc.Block())
        with self.es:
            for eng in ENGINES:
                recs = self.recs[eng]
                if not recs:
                    continue
                def make(eng, recs):
                    def body(proxy):
                        for waits, emit, sem, amt, tag in recs:
                            for s, v in sorted(waits.items()):
                                proxy.wait_ge(sems[s], v)
                            inst = emit(proxy)
                            inst.then_inc(sems[sem], amt)
                    return body
                getattr(block, eng)(make(eng, recs))
        return self.nc

    def final_waits(self, eng="gpsimd"):
        final = dict(self.cnt)
        self.recs[eng].append(({}, self._final_emit_factory(final, eng), f"C_{eng}", 1, "final"))
        self.cnt[f"C_{eng}"] += 1

    def _final_emit_factory(self, final, eng):
        def emit(proxy):
            for s, v in sorted(final.items()):
                if v > 0 and s != f"C_{eng}":
                    proxy.wait_ge(self._semh[s], v)
            return proxy.nop()
        return emit


"""Per-launch program builders for the mk_recurrent_d4 module (v1, unfused).

Conventions:
- token-major (tm): [tokens on partitions, features on free]; per-core T tokens.
- feature-major (fm): x stored in DRAM tiled as [K/128, 128, T].
- Weights for out = W @ x: host stages lhsT tiles [K/128, M/128, 128, 128] (= W.T tiled).
- Host does all relayouts (slice/transpose/retile) between launches.
"""


def cdiv(a, b):
    return (a + b - 1) // b


def tile_lhsT(wT):
    K, M = wT.shape
    out = np.empty((M // 128, 128, K), wT.dtype)
    for mt in range(M // 128):
        for kc in range(K // 128):
            out[mt, :, kc * 128:(kc + 1) * 128] = wT[kc * 128:(kc + 1) * 128,
                                                     mt * 128:(mt + 1) * 128]
    return np.ascontiguousarray(out)


def tile_fm(x):
    K, T = x.shape
    return np.ascontiguousarray(x.reshape(K // 128, 128, T))


def untile_fm(x):
    k, p, T = x.shape
    return x.reshape(k * p, T)


def build_ln(T=2048, D=1024, eps=1e-5, gamma_beta=False, name="ln"):
    P = Prog(name)
    x = P.din("x", [T, D], F32)
    o = P.dout("o", [T, D], F32)
    ob = P.dout("ob", [T, D], BF16)
    if gamma_beta:
        gam = P.din("gam", [2, D], F32)
        s_g1 = P.sb("s_g1", [2, D], F32)
        s_g2 = P.sb("s_g2", [2, D], F32)
        P.memset("vector", s_g1[:], 0.0)
        P.memset("vector", s_g2[:], 0.0)
        P.dma("sync", s_g1[0:1, :], gam[0:1, :])
        P.dma("sync", s_g2[0:1, :], gam[1:2, :])
        s_gamr = P.sb("s_gamr", [128, D], F32)
        s_betr = P.sb("s_betr", [128, D], F32)
        p_b = P.ps("p_b", [128, 512])
        for j in range(D // 512):
            sl = slice(j * 512, (j + 1) * 512)
            P.mm(p_b[:], [(P.bcrow("r0"), s_g1[:, sl])], tag="bcg")
            P.scopy(s_gamr[:, sl], p_b[:])
            P.mm(p_b[:], [(P.bcrow("r0"), s_g2[:, sl])], tag="bcb")
            P.scopy(s_betr[:, sl], p_b[:])
    nt = T // 128
    s_x = P.sb("s_x", [128, 2 * D], F32)
    s_sq = P.sb("s_sq", [128, D], F32)
    s_c1 = P.sb("s_c1", [128, 1], F32)
    s_c2 = P.sb("s_c2", [128, 1], F32)
    s_c3 = P.sb("s_c3", [128, 1], F32)
    s_o = P.sb("s_o", [128, D], F32)
    s_ob = P.sb("s_ob", [128, D], BF16)
    for i in range(nt):
        r = slice(i * 128, (i + 1) * 128)
        xb = s_x[:, (i % 2) * D:(i % 2 + 1) * D]
        P.dma("sync", xb, x[r, :])
        P.act(s_sq[:], xb, AF.Square)
        P.red("vector", s_c1[:], xb, ALU.add)
        P.ts("vector", s_c1[:], s_c1[:], 1.0 / D, None, op0=ALU.mult)
        P.red("vector", s_c2[:], s_sq[:], ALU.add)
        P.ts("vector", s_c2[:], s_c2[:], 1.0 / D, None, op0=ALU.mult)
        P.tt("vector", s_c3[:], s_c1[:], s_c1[:], ALU.mult)
        P.tt("vector", s_c2[:], s_c2[:], s_c3[:], ALU.subtract)
        P.ts("vector", s_c2[:], s_c2[:], eps, None, op0=ALU.add)
        P.act(s_c2[:], s_c2[:], AF.Sqrt)
        P.op("vector", lambda e, a=s_c3, b=s_c2: e.reciprocal(a[:], b[:]),
             reads=[s_c2[:]], writes=[s_c3[:]])
        P.ts("vector", s_o[:], xb, s_c1[:], s_c3[:], op0=ALU.subtract, op1=ALU.mult)
        if gamma_beta:
            P.tt("vector", s_o[:], s_o[:], s_gamr[:], ALU.mult)
            P.tt("vector", s_o[:], s_o[:], s_betr[:], ALU.add)
        P.dma("gpsimd", o[r, :], s_o[:])
        P.copy("vector", s_ob[:], s_o[:])
        P.dma("gpsimd", ob[r, :], s_ob[:])
    P.final_waits("gpsimd")
    return P.finalize()


def build_lin(name, K, M, T=2048, *, x_in="bf16", cast="bf16",
              epi="copy", out_dt="f32", bias=False, scale=None,
              scale_col=False, scale_row=False, TCn=512):
    P = Prog(name)
    kc_n, mt_n, ntc = K // 128, M // 128, T // TCn
    xdt = {"f32": F32, "bf16": BF16, "f8": F8E5}[x_in]
    cdt = {"bf16": BF16, "f8": F8E5}[cast]
    odt = {"f32": F32, "bf16": BF16}[out_dt]
    x = P.din("x", [kc_n, 128, T], xdt)
    w = P.din("w", [mt_n, 128, K], F32)
    out = P.dout("out", [mt_n, 128, T], odt)
    epi_f = {"copy": AF.Identity, "relu": AF.Relu, "silu": AF.Silu, "softplus": None,
             "sigmoid": AF.Sigmoid, "exp": AF.Exp, "gelu": AF.Gelu}[epi]

    if xdt != cdt:
        s_xst = P.sb("s_xst", [128, 2 * T], xdt)
        s_xc = P.sb("s_xc", [128, kc_n * T], cdt)
        for kc in range(kc_n):
            st_ = s_xst[:, (kc % 2) * T:(kc % 2 + 1) * T]
            P.dma("sync", st_, x[kc])
            P.copy("vector", s_xc[:, kc * T:(kc + 1) * T], st_)
    else:
        s_xc = P.sb("s_x", [128, kc_n * T], xdt)
        for kc in range(kc_n):
            P.dma("sync", s_xc[:, kc * T:(kc + 1) * T], x[kc])

    def xt(kc, tc):
        return s_xc[:, kc * T + tc * TCn: kc * T + (tc + 1) * TCn]

    if bias:
        b = P.din("b", [mt_n, 128, 1], F32)
        s_b = P.sb("s_b", [128, mt_n], F32)
        for mt in range(mt_n):
            P.dma("sync", s_b[:, mt:mt + 1], b[mt])
        if cast == "f8":
            s_bq = P.sb("s_bq", [128, mt_n], F8E5)
            s_bf = P.sb("s_bf", [128, mt_n], F32)
            P.copy("vector", s_bq[:], s_b[:])
            P.copy("vector", s_bf[:], s_bq[:])
            s_b = s_bf
    if scale_col:
        scol = P.din("scol", [mt_n, 128, 1], F32)
        s_scol = P.sb("s_scol", [128, mt_n], F32)
        for mt in range(mt_n):
            P.dma("sync", s_scol[:, mt:mt + 1], scol[mt])
    if scale_row:
        srow = P.din("srow", [1, T], F32)
        s_srow = P.sb("s_srow", [2, T], F32)
        P.memset("vector", s_srow[:], 0.0)
        P.dma("sync", s_srow[0:1, :], srow)
        p_srow = P.ps("p_srow", [128, TCn])
        s_srowr = P.sb("s_srowr", [128, T], F32)
        for tc in range(ntc):
            sl = slice(tc * TCn, (tc + 1) * TCn)
            P.bcast_row(p_srow[:], s_srow[:, sl])
            P.scopy(s_srowr[:, sl], p_srow[:])

    WB = kc_n * 128
    s_wst = P.sb("s_wst", [128, 2 * WB], F32)
    s_wc = P.sb("s_wc", [128, 2 * WB], cdt)
    p_o = P.ps("p_o", [128, 2 * TCn])
    s_o = P.sb("s_o", [128, TCn], odt)
    if epi == "softplus":
        s_sp1 = P.sb("s_sp1", [128, TCn], F32)
        s_sp2 = P.sb("s_sp2", [128, TCn], F32)
    for mt in range(mt_n):
        wb = (mt % 2) * WB
        P.dma("sync", s_wst[:, wb:wb + WB], w[mt])
        for kc in range(kc_n):
            P.copy("vector", s_wc[:, wb + kc * 128:wb + (kc + 1) * 128],
                   s_wst[:, wb + kc * 128:wb + (kc + 1) * 128])
        for tc in range(ntc):
            po = p_o[:, (tc % 2) * TCn:(tc % 2 + 1) * TCn]
            P.mm(po, [(s_wc[:, wb + kc * 128:wb + (kc + 1) * 128], xt(kc, tc))
                      for kc in range(kc_n)], tag=f"lin{mt}")
            sc = s_scol[:, mt:mt + 1] if scale_col else (P.constf(scale) if scale else 1.0)
            bi = s_b[:, mt:mt + 1] if bias else 0.0
            if epi == "softplus":
                P.act(s_sp1[:], po, AF.Identity, bias=bi)
                P.ts("vector", s_sp2[:], s_sp1[:], -1.0, None, op0=ALU.mult)
                P.tt("vector", s_sp2[:], s_sp2[:], s_sp1[:], ALU.max)
                P.act(s_sp2[:], s_sp2[:], AF.Exp, scale=P.constf(-1.0))
                P.act(s_sp2[:], s_sp2[:], AF.Ln, bias=P.constf(1.0))
                P.ts("vector", s_sp1[:], s_sp1[:], 0.0, None, op0=ALU.max)
                P.tt("vector", s_o[:], s_sp1[:], s_sp2[:], ALU.add)
            else:
                P.act(s_o[:], po, epi_f, scale=sc, bias=bi)
            if scale_row:
                P.tt("vector", s_o[:], s_o[:], s_srowr[:, tc * TCn:(tc + 1) * TCn], ALU.mult)
            P.dma("gpsimd", out[mt, :, tc * TCn:(tc + 1) * TCn], s_o[:])
    P.final_waits("gpsimd")
    return P.finalize()


def build_amax(T=2048, D=1024, name="amax"):
    P = Prog(name)
    x = P.din("x", [T, D], F32)
    o = P.dout("o", [1, 1], F32)
    s_x = P.sb("s_x", [128, 2 * D], F32)
    s_m = P.sb("s_m", [128, 1], F32)
    s_acc = P.sb("s_acc", [128, 1], F32)
    s_o = P.sb("s_o", [1, 2], F32)
    P.memset("vector", s_acc[:], 0.0)
    for i in range(T // 128):
        xb = s_x[:, (i % 2) * D:(i % 2 + 1) * D]
        P.dma("sync", xb, x[i * 128:(i + 1) * 128, :])
        P.red("vector", s_m[:], xb, ALU.max, absval=True)
        P.tt("vector", s_acc[:], s_acc[:], s_m[:], ALU.max)
    P.red("gpsimd", s_o[0:1, 0:1], s_acc[:], ALU.max, axis=AX.C)
    P.dma("gpsimd", o, s_o[0:1, 0:1])
    P.final_waits("gpsimd")
    return P.finalize()


def build_rope(H, dh, T=2048, RD=64, name="rope"):
    P = Prog(name)
    qn = H * dh // 128
    q = P.din("q", [qn, 128, T], BF16)
    cos = P.din("cos", [RD, T], F32)
    sin = P.din("sin", [RD, T], F32)
    o = P.dout("o", [H, RD, T], BF16)
    s_q = P.sb("s_q", [128, qn * T], BF16)
    for i in range(qn):
        P.dma("sync", s_q[:, i * T:(i + 1) * T], q[i])
    s_cs = P.sb("s_cs", [RD, 2 * T], F32)
    P.dma("sync", s_cs[:, 0:T], cos)
    P.dma("sync", s_cs[:, T:2 * T], sin)
    TCn = 512
    s_x = P.sb("s_xr", [RD, TCn], F32)
    s_rot = P.sb("s_rot", [RD, TCn], F32)
    s_t1 = P.sb("s_t1", [RD, TCn], F32)
    s_ob = P.sb("s_obr", [RD, TCn], BF16)
    hf = RD // 2
    for h in range(H):
        base_tile = (h * dh) // 128
        off = (h * dh) % 128
        for tc in range(T // TCn):
            src = s_q[off:off + RD, base_tile * T + tc * TCn: base_tile * T + (tc + 1) * TCn]
            cosr = s_cs[:, tc * TCn:(tc + 1) * TCn]
            sinr = s_cs[:, T + tc * TCn:T + (tc + 1) * TCn]
            P.copy("vector", s_x[:], src)
            P.ts("vector", s_rot[0:hf, :], s_x[hf:RD, :], -1.0, None, op0=ALU.mult)
            P.copy("vector", s_rot[hf:RD, :], s_x[0:hf, :])
            P.tt("vector", s_t1[:], s_x[:], cosr, ALU.mult)
            P.tt("vector", s_rot[:], s_rot[:], sinr, ALU.mult)
            P.tt("vector", s_t1[:], s_t1[:], s_rot[:], ALU.add)
            P.copy("vector", s_ob[:], s_t1[:])
            P.dma("gpsimd", o[h, :, tc * TCn:(tc + 1) * TCn], s_ob[:])
    P.final_waits("gpsimd")
    return P.finalize()


def build_att(name, H, dh, kvH, T=2048, S=8192, causal=False, gate=False, dbg=False, q0=None):
    P = Prog(name)
    qn = H * dh // 128
    G = H // kvH
    AUG = dh + 2
    nkb = S // 128
    scale = 1.0 / float(np.sqrt(dh))
    q = P.din("q", [qn, 128, T], BF16)
    kf = P.din("kf", [kvH * dh // 128, 128, S], BF16)
    va = P.din("va", [nkb, 128, kvH * AUG], BF16)
    if causal:
        maskq = P.din("maskq", [T // 128, S // 512, 128, 512], BF16)
        maskt = P.din("maskt", [T // 512, nkb, 128, 512], BF16)
    if gate:
        gt = P.din("gt", [qn, 128, T], BF16)
    out = P.dout("out", [qn, 128, T], BF16)

    s_q = P.sb("s_q", [128, qn * T], BF16)
    for i in range(qn):
        P.dma("sync", s_q[:, i * T:(i + 1) * T], q[i])
    ktiles = dh // 128 if dh >= 128 else 1
    s_kw = P.sb("s_kw", [128, max(2, ktiles) * S], BF16)
    s_vh = P.sb("s_vh", [128, nkb * AUG], BF16)
    s_mq = P.sb("s_mq", [128, 2 * 512], BF16) if causal else None
    s_mt = P.sb("s_mt", [128, 2 * 512], BF16) if causal else None
    s_sc = P.sb("s_sc", [128, S], F32)
    s_mcol = P.sb("s_mcol", [128, T // 128], F32)
    s_mcolb = P.sb("s_mcolb", [128, T // 128], BF16)
    s_mrowT = P.sb("s_mrowT", [T // 128, 128], F32)
    p_mt = P.ps("p_mt", [T // 128, 128], BF16)
    s_mr = P.sb("s_mr", [128, 512], F32)
    s_e = P.sb("s_e", [128, 512], F32)
    s_p = P.sb("s_p", [128, 2 * 512], BF16)
    s_lst = P.sb("s_lst", [2, 512], F32)
    s_linv = P.sb("s_linv", [2, 512], F32)
    s_lr = P.sb("s_lr", [128, 512], F32)
    s_gtb = P.sb("s_gtb", [128, 512], BF16) if gate else None
    s_gts = P.sb("s_gts", [128, 512], F32) if gate else None
    s_ctx = P.sb("s_ctx", [128, qn * 512], BF16)
    p_s = P.ps("p_s", [128, 2 * 512])
    nctx = 1 if dh == 64 else 2
    p_ctx = [P.ps(f"p_ctx{i}", [128, 512]) for i in range(nctx)]
    p_l = P.ps("p_l", [2, 512]) if dh != 64 else None
    P.memset("vector", s_linv[:], 0.0)

    def qsl(h, r0, r1, t0, t1):
        a = h * dh + r0
        tile, off = a // 128, a % 128
        return s_q[off:off + (r1 - r0), tile * T + t0: tile * T + t1]

    def ksl(r0, r1, t0, t1, kvh):
        off = r0 % 128
        return s_kw[off:off + (r1 - r0), t0:t1]

    for h in range(H):
        kvh = h // G
        if dh == 64:
            P.dma("sync", s_kw[:, 0:S], kf[(kvh * dh) // 128])
            for kb in range(nkb):
                P.dma("sync", s_vh[:, kb * AUG:(kb + 1) * AUG],
                      va[kb][:, kvh * AUG:(kvh + 1) * AUG])
        elif h % G == 0:
            for j in range(ktiles):
                P.dma("sync", s_kw[:, j * S:(j + 1) * S], kf[kvh * ktiles + j])
            for kb in range(nkb):
                P.dma("sync", s_vh[:, kb * AUG:(kb + 1) * AUG],
                      va[kb][:, kvh * AUG:(kvh + 1) * AUG])
        koff = (kvh * dh) % 128 if dh == 64 else 0
        for qt in range(T // 128):
            if causal and q0 is not None:
                kb1_hi = min(S // 512, (q0 + qt * 128 + 127) // 512 + 1)
                diag1_lo = (q0 + qt * 128) // 512
            else:
                kb1_hi = S // 512
                diag1_lo = None
            for kb in range(kb1_hi):
                ps = p_s[:, (kb % 2) * 512:(kb % 2 + 1) * 512]
                if dh == 64:
                    pairs = [(qsl(h, 0, 64, qt * 128, (qt + 1) * 128),
                              ksl(koff, koff + 64, kb * 512, (kb + 1) * 512, kvh))]
                else:
                    pairs = [(qsl(h, j * 128, (j + 1) * 128, qt * 128, (qt + 1) * 128),
                              s_kw[:, j * S + kb * 512: j * S + (kb + 1) * 512])
                             for j in range(ktiles)]
                P.mm(ps, pairs, tag="qk1")
                dst = s_sc[:, kb * 512:(kb + 1) * 512]
                if causal and (diag1_lo is None or kb >= diag1_lo):
                    mb = s_mq[:, (kb % 2) * 512:(kb % 2 + 1) * 512]
                    P.dma("scalar", mb, maskq[qt, kb])
                    P.tt("vector", dst, ps, mb, ALU.add)
                else:
                    P.scopy(dst, ps)
            P.red("vector", s_mcol[:, qt:qt + 1], s_sc[:, 0:kb1_hi * 512], ALU.max)
        P.ts("vector", s_mcolb[:], s_mcol[:], -scale, None, op0=ALU.mult)
        P.transpose(p_mt[:], s_mcolb[:], P.iden())
        P.scopy(s_mrowT[:], p_mt[:])
        for qc in range(T // 512):
            for j in range(4):
                qt = qc * 4 + j
                P.mm(p_s[:, j * 128:(j + 1) * 128],
                     [(P.sel16()[0:T // 128, qt * 128:(qt + 1) * 128], s_mrowT[:])], tag="bcm")
            P.scopy(s_mr[:], p_s[:, 0:512])
            if causal and q0 is not None:
                kb2_hi = min(nkb, (q0 + qc * 512 + 511) // 128 + 1)
                diag2_lo = (q0 + qc * 512) // 128
            else:
                kb2_hi = nkb
                diag2_lo = None
            for kb in range(kb2_hi):
                ps = p_s[:, 512:1024]
                if dh == 64:
                    pairs = [(ksl(koff, koff + 64, kb * 128, (kb + 1) * 128, kvh),
                              qsl(h, 0, 64, qc * 512, (qc + 1) * 512))]
                else:
                    pairs = [(s_kw[:, j * S + kb * 128: j * S + kb * 128 + 128],
                              qsl(h, j * 128, (j + 1) * 128, qc * 512, (qc + 1) * 512))
                             for j in range(ktiles)]
                P.mm(ps, pairs, tag="qk2")
                if causal and (diag2_lo is None or kb >= diag2_lo):
                    mb = s_mt[:, (kb % 2) * 512:(kb % 2 + 1) * 512]
                    P.dma("scalar", mb, maskt[qc, kb])
                    P.tt("vector", ps, ps, mb, ALU.add)
                P.stt("vector", s_e[:], ps, scale, s_mr[:], op0=ALU.mult, op1=ALU.add)
                pb = s_p[:, (kb % 2) * 512:(kb % 2 + 1) * 512]
                P.act(pb, s_e[:], AF.Exp)
                st = (kb == 0)
                sp = (kb == kb2_hi - 1)
                vb = kb * AUG
                if dh == 64:
                    P.mm_acc(p_ctx[0][0:AUG, :], [(s_vh[:, vb:vb + AUG], pb)],
                             start=st, stop=sp, tag="av")
                else:
                    P.mm_acc(p_ctx[0][:, :], [(s_vh[:, vb:vb + 128], pb)],
                             start=st, stop=sp, tag="av")
                    P.mm_acc(p_ctx[1][:, :], [(s_vh[:, vb + 128:vb + 256], pb)],
                             start=st, stop=sp, tag="av2")
                    P.mm_acc(p_l[:, :], [(s_vh[:, vb + 256:vb + 258], pb)],
                             start=st, stop=sp, tag="avl")
            qt0 = (h * dh) // 128
            P.scopy(s_lst[0:2, :], p_ctx[0][64:66, :] if dh == 64 else p_l[0:2, :])
            P.op("vector", lambda e: e.reciprocal(s_linv[0:1, :], s_lst[0:1, :]),
                 reads=[s_lst[:]], writes=[s_linv[:]], tag="recip")
            P.mm(p_s[:, 0:512], [(P.bcrow("r0"), s_linv[:])], tag="bcl")
            P.scopy(s_lr[:], p_s[:, 0:512])
            qs = slice(qc * 512, (qc + 1) * 512)
            if dh == 64:
                off = (h * dh) % 128
                dst = s_ctx[off:off + 64, qt0 * 512:(qt0 + 1) * 512]
                P.tt("vector", dst, p_ctx[0][0:64, :], s_lr[0:64, :], ALU.mult)
            else:
                for j in range(2):
                    dst = s_ctx[:, (qt0 + j) * 512:(qt0 + j + 1) * 512]
                    P.tt("vector", dst, p_ctx[j][:, :], s_lr[:], ALU.mult)
                    if gate:
                        P.dma("scalar", s_gtb[:], gt[qt0 + j][:, qs])
                        P.act(s_gts[:], s_gtb[:], AF.Silu)
                        P.tt("vector", dst, dst, s_gts[:], ALU.mult)
            if dh == 64:
                off = (h * dh) % 128
                P.dma("gpsimd", out[qt0][off:off + 64, qs],
                      s_ctx[off:off + 64, qt0 * 512:(qt0 + 1) * 512])
            else:
                for j in range(2):
                    P.dma("gpsimd", out[qt0 + j][:, qs],
                          s_ctx[:, (qt0 + j) * 512:(qt0 + j + 1) * 512])
    P.final_waits("gpsimd")
    return P.finalize()


def build_scan(name, DC=512, S=8192, N=16, CH=512):
    P = Prog(name)
    ndt = DC // 128
    nch = S // CH
    u = P.din("u", [ndt, 128, S], BF16)
    dt = P.din("dt", [ndt, 128, S], BF16)
    Brow = P.din("Brow", [16, S], F32)
    Crow = P.din("Crow", [16, S], F32)
    Alog = P.din("Alog", [ndt, 128, N], F32)
    Dcol = P.din("Dcol", [ndt, 128, 1], F32)
    y = P.dout("y", [ndt, 128, S], F32)

    s_B = P.sb("s_B", [16, S], F32)
    s_C = P.sb("s_C", [16, S], F32)
    P.dma("sync", s_B[:], Brow)
    P.dma("sync", s_C[:], Crow)
    s_A = P.sb("s_A", [128, ndt * N], F32)
    s_D = P.sb("s_D", [128, ndt], F32)
    for d in range(ndt):
        P.dma("sync", s_A[:, d * N:(d + 1) * N], Alog[d])
        P.dma("sync", s_D[:, d:d + 1], Dcol[d])
    P.act(s_A[:], s_A[:], AF.Exp)
    P.ts("vector", s_A[:], s_A[:], -1.0, None, op0=ALU.mult)
    s_carry = P.sb("s_carry", [128, ndt * N], F32)
    P.memset("vector", s_carry[:], 0.0)

    s_u = P.sb("s_u", [128, 2 * S], BF16)
    s_dt = P.sb("s_dt", [128, 2 * S], BF16)
    s_w = P.sb("s_w", [128, CH], F32)
    s_a = P.sb("s_a", [128, CH], F32)
    s_b = P.sb("s_b", [128, CH], F32)
    s_h = P.sb("s_h", [128, CH], F32)
    s_pst = P.sb("s_pst", [128, CH * N], F32)
    s_y = P.sb("s_y", [128, CH], F32)
    p_bc = P.ps("p_bc", [128, 2 * CH])
    pst3 = s_pst[:].rearrange("p (t n) -> p t n", n=N)

    for d in range(ndt):
        db = (d % 2) * S
        P.dma("sync", s_u[:, db:db + S], u[d])
        P.dma("sync", s_dt[:, db:db + S], dt[d])
        for ch in range(nch):
            cs = slice(db + ch * CH, db + (ch + 1) * CH)
            P.tt("vector", s_w[:], s_dt[:, cs], s_u[:, cs], ALU.mult)
            for n in range(N):
                P.mm(p_bc[:, 0:CH], [(P.sel16()[:, n * 128:(n + 1) * 128], s_B[:, ch * CH:(ch + 1) * CH])], tag="bcB")
                P.mm(p_bc[:, CH:2 * CH], [(P.sel16()[:, n * 128:(n + 1) * 128], s_C[:, ch * CH:(ch + 1) * CH])], tag="bcC")
                P.act(s_a[:], s_dt[:, cs], AF.Exp, scale=s_A[:, d * N + n:d * N + n + 1])
                P.tt("vector", s_b[:], s_w[:], p_bc[:, 0:CH], ALU.mult)
                P.scan("vector", s_h[:], s_a[:], s_b[:], s_carry[:, d * N + n:d * N + n + 1])
                P.copy("vector", s_carry[:, d * N + n:d * N + n + 1], s_h[:, CH - 1:CH])
                P.tt("vector", pst3[:, :, n], s_h[:], p_bc[:, CH:2 * CH], ALU.mult)
            P.red("vector", s_y[:], pst3, ALU.add, axis=AX.X)
            P.stt("vector", s_y[:], s_u[:, cs], s_D[:, d:d + 1], s_y[:], op0=ALU.mult, op1=ALU.add)
            P.dma("gpsimd", y[d][:, ch * CH:(ch + 1) * CH], s_y[:])
    P.final_waits("gpsimd")
    return P.finalize()


def build_conv3(DC=512, S=8192, KW=4, name="conv3"):
    P = Prog(name)
    ndt = DC // 128
    CH = 512
    xi = P.din("xi", [ndt, 128, S], BF16)
    cw = P.din("cw", [ndt, 128, KW], F32)
    cb = P.din("cb", [ndt, 128, 1], F32)
    xo = P.dout("xo", [ndt, 128, S], BF16)
    s_x = P.sb("s_x", [128, 2 * S], BF16)
    s_cw = P.sb("s_cw", [128, ndt * KW], F32)
    s_cb = P.sb("s_cb", [128, ndt], F32)
    s_acc = P.sb("s_acc", [128, CH], F32)
    s_o = P.sb("s_o", [128, CH], BF16)
    for d in range(ndt):
        P.dma("sync", s_cw[:, d * KW:(d + 1) * KW], cw[d])
        P.dma("sync", s_cb[:, d:d + 1], cb[d])
    for d in range(ndt):
        db = (d % 2) * S
        P.dma("sync", s_x[:, db:db + S], xi[d])
        for ch in range(S // CH):
            t0 = ch * CH
            P.memset("vector", s_acc[:], 0.0)
            for k in range(KW):
                sh = k - (KW - 1)
                lo = max(0, -(t0 + sh))
                src = s_x[:, db + t0 + sh + lo: db + t0 + sh + CH]
                P.stt("vector", s_acc[:, lo:CH], src, s_cw[:, d * KW + k:d * KW + k + 1],
                      s_acc[:, lo:CH], op0=ALU.mult, op1=ALU.add)
            P.act(s_o[:], s_acc[:], AF.Silu, bias=s_cb[:, d:d + 1])
            P.dma("gpsimd", xo[d][:, t0:t0 + CH], s_o[:])
    P.final_waits("gpsimd")
    return P.finalize()


def build_router(T=2048, D=1024, E=8, name="router"):
    P = Prog(name)
    kc_n = D // 128
    xn = P.din("x", [kc_n, 128, T], BF16)
    rw = P.din("rw", [kc_n, 128, 16], F32)
    ge_o = P.dout("ge", [T, E], F32)
    s_x = P.sb("s_x", [128, kc_n * T], BF16)
    for kc in range(kc_n):
        P.dma("sync", s_x[:, kc * T:(kc + 1) * T], xn[kc])
    s_rw = P.sb("s_rw", [128, kc_n * 16], F32)
    s_rwb = P.sb("s_rwb", [128, kc_n * 16], BF16)
    for kc in range(kc_n):
        P.dma("sync", s_rw[:, kc * 16:(kc + 1) * 16], rw[kc])
    P.copy("vector", s_rwb[:], s_rw[:])
    p_lg = P.ps("p_lg", [128, 16])
    s_lg = P.sb("s_lg", [128, 16], F32)
    s_m1 = P.sb("s_m1", [128, 1], F32)
    s_m2 = P.sb("s_m2", [128, 1], F32)
    s_sig = P.sb("s_sig", [128, 2], F32)
    s_eq1 = P.sb("s_eq1", [128, E], F32)
    s_eq2 = P.sb("s_eq2", [128, E], F32)
    s_msk = P.sb("s_msk", [128, E], F32)
    s_gs = P.sb("s_gs", [128, 1], F32)
    s_gi = P.sb("s_gi", [128, 1], F32)
    s_ge = P.sb("s_ge", [128, E], F32)
    for i in range(T // 128):
        P.mm(p_lg[:], [(s_x[:, kc * T + i * 128: kc * T + (i + 1) * 128],
                        s_rwb[:, kc * 16:(kc + 1) * 16]) for kc in range(kc_n)], tag="lg")
        P.scopy(s_lg[:], p_lg[:])
        lg = s_lg[:, 0:E]
        P.red("vector", s_m1[:], lg, ALU.max)
        P.ts("vector", s_eq1[:], lg, s_m1[:], None, op0=ALU.is_equal)
        P.stt("vector", s_msk[:], s_eq1[:], -1e30, lg, op0=ALU.mult, op1=ALU.add)
        P.red("vector", s_m2[:], s_msk[:], ALU.max)
        P.ts("vector", s_eq2[:], lg, s_m2[:], None, op0=ALU.is_equal)
        P.copy("vector", s_sig[:, 0:1], s_m1[:])
        P.copy("vector", s_sig[:, 1:2], s_m2[:])
        P.act(s_sig[:], s_sig[:], AF.Sigmoid)
        P.tt("vector", s_gs[:], s_sig[:, 0:1], s_sig[:, 1:2], ALU.add)
        P.ts("vector", s_gs[:], s_gs[:], 1e-10, None, op0=ALU.add)
        P.op("vector", lambda e: e.reciprocal(s_gi[:], s_gs[:]), reads=[s_gs[:]], writes=[s_gi[:]])
        P.ts("vector", s_eq1[:], s_eq1[:], s_sig[:, 0:1], s_gi[:], op0=ALU.mult, op1=ALU.mult)
        P.ts("vector", s_eq2[:], s_eq2[:], s_sig[:, 1:2], s_gi[:], op0=ALU.mult, op1=ALU.mult)
        P.tt("vector", s_ge[:], s_eq1[:], s_eq2[:], ALU.add)
        P.dma("gpsimd", ge_o[i * 128:(i + 1) * 128, :], s_ge[:])
    P.final_waits("gpsimd")
    return P.finalize()


def build_moe(T=2048, D=1024, HH=2048, E=8, name="moe"):
    P = Prog(name)
    kc_n = D // 128
    mt_n = HH // 128
    ot_n = D // 128
    TH = 1024
    x = P.din("x", [kc_n, 128, T], BF16)
    ge = P.din("ge", [16, T], F32)
    wg = P.din("wg", [E, mt_n, 128, D], F32)
    wu = P.din("wu", [E, mt_n, 128, D], F32)
    wd = P.din("wd", [E, ot_n, 128, HH], F32)
    g_o = P.dout("g", [ot_n, 128, T], F32)
    s_x = P.sb("s_x", [128, kc_n * T], BF16)
    for kc in range(kc_n):
        P.dma("sync", s_x[:, kc * T:(kc + 1) * T], x[kc])
    s_ge = P.sb("s_ge", [16, T], F32)
    P.dma("sync", s_ge[:], ge)
    s_wst = P.sb("s_wst", [128, 2 * D], F32)
    s_wgb = P.sb("s_wgb", [128, 2 * D], BF16)
    s_wust = P.sb("s_wust", [128, 2 * D], F32)
    s_wub = P.sb("s_wub", [128, 2 * D], BF16)
    s_wdst = P.sb("s_wdst", [128, 2 * HH], F32)
    s_wdb = P.sb("s_wdb", [128, 2 * HH], BF16)
    s_h = P.sb("s_h", [128, mt_n * 1024], BF16)
    s_sg = P.sb("s_sg", [128, 512], F32)
    s_ger = P.sb("s_ger", [128, 1024], F32)
    s_tmp = P.sb("s_tmp", [128, 512], F32)
    s_acc = P.sb("s_acc", [128, ot_n * TH], F32)
    s_go = P.sb("s_go", [128, 512], F32)
    p_g = P.ps("p_g", [128, 512])
    p_u = P.ps("p_u", [128, 512])
    p_d = P.ps("p_d", [128, 2 * 512])
    p_ger = P.ps("p_ger", [128, 512])

    def xt(kc, th, tc):
        t0 = th * TH + tc * 512
        return s_x[:, kc * T + t0: kc * T + t0 + 512]

    for th in range(T // TH):
        P.memset("vector", s_acc[:], 0.0)
        for e in range(E):
            for tc in range(TH // 512):
                t0 = th * TH + tc * 512
                P.mm(p_ger[:], [(P.sel16()[:, e * 128:(e + 1) * 128], s_ge[:, t0:t0 + 512])], tag="bcge")
                P.scopy(s_ger[:, tc * 512:(tc + 1) * 512], p_ger[:])
            for mt in range(mt_n):
                wb = (mt % 2) * D
                P.dma("sync", s_wst[:, wb:wb + D], wg[e, mt])
                P.copy("vector", s_wgb[:, wb:wb + D], s_wst[:, wb:wb + D])
                P.dma("sync", s_wust[:, wb:wb + D], wu[e, mt])
                P.copy("vector", s_wub[:, wb:wb + D], s_wust[:, wb:wb + D])
                for tc in range(TH // 512):
                    P.mm(p_g[:], [(s_wgb[:, wb + kc * 128:wb + (kc + 1) * 128], xt(kc, th, tc))
                                  for kc in range(kc_n)], tag="moeg")
                    P.mm(p_u[:], [(s_wub[:, wb + kc * 128:wb + (kc + 1) * 128], xt(kc, th, tc))
                                  for kc in range(kc_n)], tag="moeu")
                    P.act(s_sg[:], p_g[:], AF.Silu)
                    P.tt("vector", s_h[:, mt * TH + tc * 512: mt * TH + (tc + 1) * 512],
                         s_sg[:], p_u[:], ALU.mult)
            for ot in range(ot_n):
                wb = (ot % 2) * HH
                P.dma("sync", s_wdst[:, wb:wb + HH], wd[e, ot])
                P.copy("vector", s_wdb[:, wb:wb + HH], s_wdst[:, wb:wb + HH])
                for tc in range(TH // 512):
                    pd = p_d[:, (tc % 2) * 512:(tc % 2 + 1) * 512]
                    P.mm(pd, [(s_wdb[:, wb + ht * 128:wb + (ht + 1) * 128],
                               s_h[:, ht * TH + tc * 512: ht * TH + (tc + 1) * 512])
                              for ht in range(mt_n)], tag="moed")
                    P.tt("vector", s_tmp[:], pd, s_ger[:, tc * 512:(tc + 1) * 512], ALU.mult)
                    acs = s_acc[:, ot * TH + tc * 512: ot * TH + (tc + 1) * 512]
                    P.tt("vector", acs, acs, s_tmp[:], ALU.add)
        for ot in range(ot_n):
            for tc in range(TH // 512):
                acs = s_acc[:, ot * TH + tc * 512: ot * TH + (tc + 1) * 512]
                P.act(s_go[:], acs, AF.Silu)
                P.dma("gpsimd", g_o[ot][:, th * TH + tc * 512: th * TH + (tc + 1) * 512], s_go[:])
    P.final_waits("gpsimd")
    return P.finalize()


def build_lin_i8(name, K, M, T=2048, silu_until=None, epi_all="silu"):
    P = Prog(name)
    kc_n, mt_n, ntc = K // 128, M // 128, T // 512
    x = P.din("x", [kc_n, 128, T], F32)
    wrm = P.din("wrm", [mt_n, 128, K], F32)
    gparts = P.din("gparts", [2, 8], F32)
    clip = P.din("clip", [2, 2], F32)
    out = P.dout("out", [mt_n, 128, T], F32)

    s_gp = P.sb("s_gp", [2, 8], F32)
    s_cl = P.sb("s_cl", [2, 2], F32)
    s_s = P.sb("s_s", [2, 2], F32)
    P.dma("sync", s_gp[:], gparts)
    P.dma("sync", s_cl[:], clip)
    P.red("vector", s_s[:, 0:1], s_gp[:], ALU.max)
    P.act(s_cl[:], s_cl[:], AF.Sigmoid)
    P.tt("vector", s_s[:, 0:1], s_s[:, 0:1], s_cl[:, 0:1], ALU.mult)
    P.ts("vector", s_s[:, 0:1], s_s[:, 0:1], 1.0 / 127.0, None, op0=ALU.mult)
    p_bc = P.ps("p_bc", [128, 2])
    s_sx = P.sb("s_sx", [128, 1], F32)
    s_sxi = P.sb("s_sxi", [128, 1], F32)
    P.mm(p_bc[:], [(P.bcrow("r0"), s_s[:, 0:2])], tag="bcs")
    P.scopy(s_sx[:], p_bc[:, 0:1])
    P.op("vector", lambda e: e.reciprocal(s_sxi[:], s_sx[:]), reads=[s_sx[:]], writes=[s_sxi[:]])

    s_xst = P.sb("s_xst", [128, 2 * T], F32)
    s_xq = P.sb("s_xq", [128, kc_n * T], BF16)
    s_xi = P.sb("s_xi", [128, 512], I32)
    s_xf = P.sb("s_xf", [128, 512], F32)
    for kc in range(kc_n):
        stg = s_xst[:, (kc % 2) * T:(kc % 2 + 1) * T]
        P.dma("sync", stg, x[kc])
        for tc in range(ntc):
            sl = slice(kc * T + tc * 512, kc * T + (tc + 1) * 512)
            P.ts("vector", s_xf[:], stg[:, tc * 512:(tc + 1) * 512], s_sxi[:], None, op0=ALU.mult)
            P.copy("vector", s_xi[:], s_xf[:])
            P.copy("vector", s_xf[:], s_xi[:])
            P.ts("vector", s_xq[:, sl], s_xf[:], 127.0, -128.0, op0=ALU.min, op1=ALU.max)

    def xt(kc, tc):
        return s_xq[:, kc * T + tc * 512: kc * T + (tc + 1) * 512]

    s_wrm = P.sb("s_wrm", [128, K], F32)
    s_wq = P.sb("s_wq", [128, K], F32)
    s_wqb = P.sb("s_wqb", [128, K], BF16)
    s_wi = P.sb("s_wi", [128, 512], I32)
    s_wmx = P.sb("s_wmx", [128, 1], F32)
    s_wsi = P.sb("s_wsi", [128, 1], F32)
    s_oc = P.sb("s_oc", [128, 1], F32)
    s_lhsT = P.sb("s_lhsT", [128, 2 * K], BF16)
    p_tr = P.ps("p_tr", [128, 128], BF16)
    p_o = P.ps("p_o", [128, 2 * 512])
    s_o = P.sb("s_o", [128, 512], F32)
    for mt in range(mt_n):
        wb = (mt % 2) * K
        P.dma("sync", s_wrm[:], wrm[mt])
        P.red("vector", s_wmx[:], s_wrm[:], ALU.max, absval=True)
        P.ts("vector", s_wmx[:], s_wmx[:], 1.0 / 127.0, None, op0=ALU.mult)
        P.op("vector", lambda e: e.reciprocal(s_wsi[:], s_wmx[:]), reads=[s_wmx[:]], writes=[s_wsi[:]])
        P.tt("vector", s_oc[:], s_wmx[:], s_sx[:], ALU.mult)
        for j in range(K // 512):
            sl = slice(j * 512, (j + 1) * 512)
            P.ts("vector", s_wq[:, sl], s_wrm[:, sl], s_wsi[:], None, op0=ALU.mult)
            P.copy("vector", s_wi[:], s_wq[:, sl])
            P.copy("vector", s_wq[:, sl], s_wi[:])
        P.ts("vector", s_wqb[:], s_wq[:], 127.0, -128.0, op0=ALU.min, op1=ALU.max)
        for kc in range(kc_n):
            P.transpose(p_tr[:], s_wqb[:, kc * 128:(kc + 1) * 128], P.iden())
            P.scopy(s_lhsT[:, wb + kc * 128:wb + (kc + 1) * 128], p_tr[:])
        fn = AF.Silu if (silu_until is None and epi_all == "silu") or (silu_until is not None and mt < silu_until) else AF.Identity
        for tc in range(ntc):
            po = p_o[:, (tc % 2) * 512:(tc % 2 + 1) * 512]
            P.mm(po, [(s_lhsT[:, wb + kc * 128:wb + (kc + 1) * 128], xt(kc, tc))
                      for kc in range(kc_n)], tag="i8mm")
            P.act(s_o[:], po, fn, scale=s_oc[:])
            P.dma("gpsimd", out[mt][:, tc * 512:(tc + 1) * 512], s_o[:])
    P.final_waits("gpsimd")
    return P.finalize()


def build_rmsq(name, D, T=2048, eps_norm=1e-8, eps_q=1e-5):
    P = Prog(name)
    x = P.din("x", [T, D], F32)
    nw = P.din("nw", [2, D], F32)
    xq = P.dout("xq", [T, D], BF16)
    srow = P.dout("srow", [1, T], F32)
    s_nw = P.sb("s_nw", [2, D], F32)
    P.dma("sync", s_nw[:], nw)
    s_nwr = P.sb("s_nwr", [128, D], F32)
    p_b = P.ps("p_b", [128, 512])
    for j in range(D // 512):
        sl = slice(j * 512, (j + 1) * 512)
        P.mm(p_b[:], [(P.bcrow("r0"), s_nw[:, sl])], tag="bcnw")
        P.scopy(s_nwr[:, sl], p_b[:])
    s_x = P.sb("s_x", [128, 2 * D], F32)
    s_sq = P.sb("s_sq", [128, D], F32)
    s_c1 = P.sb("s_c1", [128, 1], F32)
    s_c2 = P.sb("s_c2", [128, 1], F32)
    s_xh = P.sb("s_xh", [128, D], F32)
    s_a = P.sb("s_a", [128, 1], F32)
    s_st = P.sb("s_st", [128, 1], F32)
    s_si = P.sb("s_si", [128, 1], F32)
    s_xi = P.sb("s_xi", [128, 512], I32)
    s_xf = P.sb("s_xf", [128, 512], F32)
    s_q = P.sb("s_q", [128, D], BF16)
    for i in range(T // 128):
        xb = s_x[:, (i % 2) * D:(i % 2 + 1) * D]
        P.dma("sync", xb, x[i * 128:(i + 1) * 128, :])
        P.act(s_sq[:], xb, AF.Square)
        P.red("vector", s_c1[:], s_sq[:], ALU.add)
        P.ts("vector", s_c1[:], s_c1[:], 1.0 / D, eps_norm, op0=ALU.mult, op1=ALU.add)
        P.act(s_c1[:], s_c1[:], AF.Sqrt)
        P.op("vector", lambda e: e.reciprocal(s_c2[:], s_c1[:]), reads=[s_c1[:]], writes=[s_c2[:]])
        P.ts("vector", s_xh[:], xb, s_c2[:], None, op0=ALU.mult)
        P.tt("vector", s_xh[:], s_xh[:], s_nwr[:], ALU.mult)
        P.red("vector", s_a[:], s_xh[:], ALU.max, absval=True)
        P.ts("vector", s_a[:], s_a[:], eps_q, None, op0=ALU.max)
        P.op("vector", lambda e: e.reciprocal(s_st[:], s_a[:]), reads=[s_a[:]], writes=[s_st[:]])
        P.ts("vector", s_st[:], s_st[:], 127.0, None, op0=ALU.mult)
        P.ts("vector", s_si[:], s_a[:], 1.0 / 127.0, None, op0=ALU.mult)
        for j in range(D // 512):
            sl = slice(j * 512, (j + 1) * 512)
            P.ts("vector", s_xf[:], s_xh[:, sl], s_st[:], None, op0=ALU.mult)
            P.copy("vector", s_xi[:], s_xf[:])
            P.copy("vector", s_xf[:], s_xi[:])
            P.ts("vector", s_q[:, sl], s_xf[:], 127.0, -128.0, op0=ALU.min, op1=ALU.max)
        P.dma("gpsimd", xq[i * 128:(i + 1) * 128, :], s_q[:])
        P.dma("gpsimd", srow[0:1, i * 128:(i + 1) * 128], s_si[:])
    P.final_waits("gpsimd")
    return P.finalize()


def build_lin_tern(name, K, M, T=2048, post="h3"):
    P = Prog(name)
    kc_n, mt_n, ntc = K // 128, M // 128, T // 512
    x = P.din("x", [kc_n, 128, T], BF16)
    wrm = P.din("wrm", [mt_n, 128, K], F32)
    srow_i = P.din("srow", [1, T], F32)
    out = P.dout("out", [mt_n, 128, T], F32)
    if post == "h3":
        asab = P.din("asab", [2, 2], F32)
        s_asab = P.sb("s_asab", [2, 2], F32)
        P.dma("sync", s_asab[:], asab)

    s_x = P.sb("s_x", [128, kc_n * T], BF16)
    for kc in range(kc_n):
        P.dma("sync", s_x[:, kc * T:(kc + 1) * T], x[kc])

    s_wrm = P.sb("s_wrm", [128, K], F32)
    s_abs = P.sb("s_abs", [128, 512], F32)
    s_rs = P.sb("s_rs", [128, 1], F32)
    s_rsa = P.sb("s_rsa", [128, 1], F32)
    s_sum = P.sb("s_sum", [128, 1], F32)
    s_g = P.sb("s_g", [2, 2], F32)
    P.memset("vector", s_sum[:], 0.0)
    P.memset("vector", s_g[:], 0.0)
    for mt in range(mt_n):
        P.dma("sync", s_wrm[:], wrm[mt])
        for j in range(K // 512):
            sl = slice(j * 512, (j + 1) * 512)
            P.ts("vector", s_abs[:], s_wrm[:, sl], -1.0, None, op0=ALU.mult)
            P.tt("vector", s_abs[:], s_abs[:], s_wrm[:, sl], ALU.max)
            P.red("vector", s_rs[:], s_abs[:], ALU.add)
            P.tt("vector", s_sum[:], s_sum[:], s_rs[:], ALU.add)
    P.red("gpsimd", s_g[0:1, 0:1], s_sum[:], ALU.add, axis=AX.C)
    P.ts("vector", s_g[0:1, 0:1], s_g[0:1, 0:1], 1.0 / (M * K), None, op0=ALU.mult)
    P.ts("vector", s_g[0:1, 0:1], s_g[0:1, 0:1], 1e-5, None, op0=ALU.max)
    s_ws = P.sb("s_ws", [2, 2], F32)
    P.memset("vector", s_ws[:], 0.0)
    P.op("vector", lambda e: e.reciprocal(s_ws[0:1, 0:1], s_g[0:1, 0:1]),
         reads=[s_g[:]], writes=[s_ws[:]])
    p_bc = P.ps("p_bc", [128, 2])
    s_wsc = P.sb("s_wsc", [128, 1], F32)
    s_invc = P.sb("s_invc", [128, 1], F32)
    P.mm(p_bc[:], [(P.bcrow("r0"), s_ws[:, 0:2])], tag="bcws")
    P.scopy(s_wsc[:], p_bc[:, 0:1])
    P.mm(p_bc[:], [(P.bcrow("r0"), s_g[:, 0:2])], tag="bcinv")
    P.scopy(s_invc[:], p_bc[:, 0:1])
    if post == "h3":
        s_sqas = P.sb("s_sqas", [2, 2], F32)
        P.act(s_sqas[:], s_asab[:], AF.Sqrt)
        s_cp = P.sb("s_cp", [128, 1], F32)
        s_abc = P.sb("s_abc", [128, 1], F32)
        P.mm(p_bc[:], [(P.bcrow("r0"), s_sqas[:, 0:2])], tag="bcsq")
        P.scopy(s_cp[:], p_bc[:, 0:1])
        P.tt("vector", s_cp[:], s_cp[:], s_invc[:], ALU.mult)
        P.mm(p_bc[:], [(P.bcrow("r0"), s_asab[:, 1:3] if False else s_asab[:, 0:2])], tag="bcab")
        s_ab2 = P.sb("s_ab2", [2, 2], F32)
        P.memset("vector", s_ab2[:], 0.0)
        P.copy("vector", s_ab2[0:1, 0:1], s_asab[0:1, 1:2])
        P.mm(p_bc[:], [(P.bcrow("r0"), s_ab2[:, 0:2])], tag="bcab2")
        P.scopy(s_abc[:], p_bc[:, 0:1])
    s_srow = P.sb("s_srow", [2, T], F32)
    P.memset("vector", s_srow[:], 0.0)
    P.dma("sync", s_srow[0:1, :], srow_i)
    if post == "h3":
        P.tt("vector", s_srow[0:1, :], s_srow[0:1, :], s_srow[0:1, :], ALU.mult)
    s_srr = P.sb("s_srr", [128, T], F32)
    p_sr = P.ps("p_sr", [128, 512])
    for tc in range(ntc):
        sl = slice(tc * 512, (tc + 1) * 512)
        P.mm(p_sr[:], [(P.bcrow("r0"), s_srow[:, sl])], tag="bcsr")
        P.scopy(s_srr[:, sl], p_sr[:])

    s_wqb = P.sb("s_wqb", [128, K], BF16)
    s_wi = P.sb("s_wi", [128, 512], I32)
    s_lhsT = P.sb("s_lhsT", [128, 2 * K], BF16)
    p_tr = P.ps("p_tr", [128, 128], BF16)
    p_o = P.ps("p_o", [128, 2 * 512])
    s_r = P.sb("s_r", [128, 512], F32)
    s_o = P.sb("s_o", [128, 512], F32)
    for mt in range(mt_n):
        wb = (mt % 2) * K
        P.dma("sync", s_wrm[:], wrm[mt])
        for j in range(K // 512):
            sl = slice(j * 512, (j + 1) * 512)
            P.ts("vector", s_abs[:], s_wrm[:, sl], s_wsc[:], None, op0=ALU.mult)
            P.copy("vector", s_wi[:], s_abs[:])
            P.copy("vector", s_abs[:], s_wi[:])
            P.ts("vector", s_wqb[:, sl], s_abs[:], 1.0, -1.0, op0=ALU.min, op1=ALU.max)
        for kc in range(kc_n):
            P.transpose(p_tr[:], s_wqb[:, kc * 128:(kc + 1) * 128], P.iden())
            P.scopy(s_lhsT[:, wb + kc * 128:wb + (kc + 1) * 128], p_tr[:])
        for tc in range(ntc):
            sl = slice(tc * 512, (tc + 1) * 512)
            po = p_o[:, (tc % 2) * 512:(tc % 2 + 1) * 512]
            P.mm(po, [(s_lhsT[:, wb + kc * 128:wb + (kc + 1) * 128],
                       s_x[:, kc * T + tc * 512: kc * T + (tc + 1) * 512])
                      for kc in range(kc_n)], tag="tern")
            if post == "h3":
                P.act(s_r[:], po, AF.Relu)
                P.act(s_r[:], s_r[:], AF.Square, scale=s_cp[:])
                P.tt("vector", s_r[:], s_r[:], s_srr[:, sl], ALU.mult)
                P.ts("vector", s_o[:], s_r[:], s_abc[:], None, op0=ALU.add)
            else:
                P.act(s_r[:], po, AF.Identity, scale=s_invc[:])
                P.tt("vector", s_r[:], s_r[:], s_srr[:, sl], ALU.mult)
                P.act(s_o[:], s_r[:], AF.Silu)
            P.dma("gpsimd", out[mt][:, sl], s_o[:])
    P.final_waits("gpsimd")
    return P.finalize()


def build_elt_dropg(T=2048, FT=8, name="dropg"):
    P = Prog(name)
    y = P.din("y", [FT, 128, T], BF16)
    m = P.din("m", [FT, 128, T], BF16)
    g = P.dout("g", [FT, 128, T], F32)
    s_y = P.sb("s_y", [128, T], BF16)
    s_m = P.sb("s_m", [128, T], BF16)
    s_t = P.sb("s_t", [128, T], F32)
    s_g = P.sb("s_g", [128, T], F32)
    for ft in range(FT):
        P.dma("sync", s_y[:], y[ft])
        P.dma("sync", s_m[:], m[ft])
        P.tt("vector", s_t[:], s_y[:], s_m[:], ALU.mult)
        P.act(s_g[:], s_t[:], AF.Silu)
        P.dma("gpsimd", g[ft], s_g[:])
    P.final_waits("gpsimd")
    return P.finalize()


def build_elt_ysz(T=2048, FT=16, name="ysz"):
    P = Prog(name)
    y = P.din("y", [FT, 128, T], F32)
    z = P.din("z", [FT, 128, T], BF16)
    o = P.dout("o", [FT, 128, T], BF16)
    s_y = P.sb("s_y", [128, T], F32)
    s_z = P.sb("s_z", [128, T], BF16)
    s_sz = P.sb("s_sz", [128, T], F32)
    s_o = P.sb("s_o", [128, T], BF16)
    for ft in range(FT):
        P.dma("sync", s_y[:], y[ft])
        P.dma("sync", s_z[:], z[ft])
        P.act(s_sz[:], s_z[:], AF.Silu)
        P.tt("vector", s_o[:], s_y[:], s_sz[:], ALU.mult)
        P.dma("gpsimd", o[ft], s_o[:])
    P.final_waits("gpsimd")
    return P.finalize()


def build_elt_glu_skip(T=2048, FT=8, name="glusk"):
    P = Prog(name)
    gg = P.din("gg", [2 * FT, 128, T], F32)
    sk = P.din("sk", [FT, 128, T], F32)
    o = P.dout("o", [FT, 128, T], F32)
    s_1 = P.sb("s_1", [128, T], F32)
    s_2 = P.sb("s_2", [128, T], F32)
    s_k = P.sb("s_k", [128, T], F32)
    s_o = P.sb("s_o", [128, T], F32)
    for ft in range(FT):
        P.dma("sync", s_1[:], gg[ft])
        P.dma("sync", s_2[:], gg[FT + ft])
        P.dma("sync", s_k[:], sk[ft])
        P.act(s_2[:], s_2[:], AF.Sigmoid)
        P.tt("vector", s_o[:], s_1[:], s_2[:], ALU.mult)
        P.tt("vector", s_o[:], s_o[:], s_k[:], ALU.add)
        P.dma("gpsimd", o[ft], s_o[:])
    P.final_waits("gpsimd")
    return P.finalize()


def build_elt_combine(T=2048, FT=8, name="combine"):
    P = Prog(name)
    x = P.din("x", [FT, 128, T], F32)
    s = P.din("s", [FT, 128, T], F32)
    g = P.din("g", [FT, 128, T], F32)
    o = P.dout("o", [FT, 128, T], F32)
    s_x = P.sb("s_x", [128, T], F32)
    s_s = P.sb("s_s", [128, T], F32)
    s_g = P.sb("s_g", [128, T], F32)
    s_o = P.sb("s_o", [128, T], F32)
    for ft in range(FT):
        P.dma("sync", s_x[:], x[ft])
        P.dma("sync", s_s[:], s[ft])
        P.dma("sync", s_g[:], g[ft])
        P.tt("vector", s_o[:], s_s[:], s_g[:], ALU.mult)
        P.tt("vector", s_o[:], s_o[:], s_x[:], ALU.add)
        P.dma("gpsimd", o[ft], s_o[:])
    P.final_waits("gpsimd")
    return P.finalize()


def build_elt_h2(T=2048, FT=22, name="h2"):
    P = Prog(name)
    g = P.din("g", [FT, 128, T], F32)
    u = P.din("u", [FT, 128, T], F32)
    h = P.dout("h", [FT, 128, T], F32)
    am = P.dout("am", [1, 1], F32)
    s_g = P.sb("s_g", [128, T], F32)
    s_u = P.sb("s_u", [128, T], F32)
    s_h = P.sb("s_h", [128, T], F32)
    s_m = P.sb("s_m", [128, 1], F32)
    s_acc = P.sb("s_acc", [128, 1], F32)
    s_am = P.sb("s_am", [1, 2], F32)
    P.memset("vector", s_acc[:], 0.0)
    for ft in range(FT):
        P.dma("sync", s_g[:], g[ft])
        P.dma("sync", s_u[:], u[ft])
        P.tt("vector", s_h[:], s_g[:], s_u[:], ALU.mult)
        P.red("vector", s_m[:], s_h[:], ALU.max, absval=True)
        P.tt("vector", s_acc[:], s_acc[:], s_m[:], ALU.max)
        P.dma("gpsimd", h[ft], s_h[:])
    P.red("gpsimd", s_am[0:1, 0:1], s_acc[:], ALU.max, axis=AX.C)
    P.dma("gpsimd", am, s_am[0:1, 0:1])
    P.final_waits("gpsimd")
    return P.finalize()


def build_sig1(name="sig1"):
    P = Prog(name)
    og = P.din("og", [2, 2], F32)
    o = P.dout("o", [1, 1], F32)
    s = P.sb("s_og", [2, 2], F32)
    P.dma("sync", s[:], og)
    P.act(s[:], s[:], AF.Sigmoid)
    P.dma("gpsimd", o, s[0:1, 0:1])
    P.final_waits("gpsimd")
    return P.finalize()


"""Unfused multi-launch kernel for mk_recurrent_d4 (v1).

Host code between launches does ONLY layout: slicing, transposes, retiling,
concatenation, padding with constants, and broadcasting device-computed
scalars. All arithmetic on values happens on the NeuronCores.
"""
from concourse.bass_utils import run_bass_kernel_spmd

NPBF16 = ml_dtypes.bfloat16
NC = 8
T = 2048
S = 8192
D = 1024

_P = {}


def prog(name, builder, *a, **k):
    if name not in _P:
        _P[name] = builder(*a, **k)
    return _P[name]


def launch(nc, in_maps):
    res = run_bass_kernel_spmd(nc, in_maps, core_ids=list(range(NC)))
    return res.results


def tfm(a):
    return tile_fm(np.ascontiguousarray(a))


def ufm(a):
    return untile_fm(a)


def tlhsT(w):
    return tile_lhsT(np.ascontiguousarray(w))


def colM(v):
    v = np.asarray(v, np.float32)
    return np.ascontiguousarray(v.reshape(-1, 128, 1))


def sel16():
    if "sel16" not in _CONST_CACHE:
        a = np.zeros((16, 2048), np.float32)
        for n in range(16):
            a[n, n * 128:(n + 1) * 128] = 1.0
        _CONST_CACHE["sel16"] = a
    return _CONST_CACHE["sel16"]


def pad2(v00, v01=0.0):
    a = np.zeros((2, 2), np.float32)
    a[0, 0] = v00
    a[0, 1] = v01
    return a


def cores_of_b(b):
    return range(b * 4, (b + 1) * 4)


_CONST_CACHE = {}


def rope_tables():
    if "rope" not in _CONST_CACHE:
        RD = 64
        inv = (1.0 / 10000.0 ** (np.arange(0, RD, 2, dtype=np.float32) / RD)).astype(np.float32)
        pos = np.arange(S, dtype=np.float32)
        fr = pos[:, None] * inv[None, :]
        emb = np.concatenate([fr, fr], axis=-1)
        _CONST_CACHE["rope"] = (np.cos(emb).astype(np.float32).T.copy(),
                                np.sin(emb).astype(np.float32).T.copy())
    return _CONST_CACHE["rope"]


def dropout_mask():
    if "mask" not in _CONST_CACHE:
        import jax
        with jax.default_device(jax.devices("cpu")[0]):
            keep = np.asarray(jax.random.bernoulli(jax.random.key(42), 0.8, (2, S, D)))
        _CONST_CACHE["mask"] = np.where(keep, np.float32(1.25), np.float32(0.0)).astype(NPBF16)
    return _CONST_CACHE["mask"]


def causal_masks():
    if "cm" not in _CONST_CACHE:
        out = []
        for c in range(NC):
            q0 = (c % 4) * T
            mq = np.zeros((T // 128, S // 512, 128, 512), np.float32)
            for qt in range(T // 128):
                qpos = q0 + qt * 128 + np.arange(128)
                for kb in range(S // 512):
                    kpos = kb * 512 + np.arange(512)
                    mq[qt, kb][kpos[None, :] > qpos[:, None]] = -1e30
            mt = np.zeros((T // 512, S // 128, 128, 512), np.float32)
            for qc in range(T // 512):
                qpos = q0 + qc * 512 + np.arange(512)
                for kb in range(S // 128):
                    kpos = kb * 128 + np.arange(128)
                    mt[qc, kb][kpos[:, None] > qpos[None, :]] = -1e30
            out.append((mq.astype(NPBF16), mt.astype(NPBF16)))
        _CONST_CACHE["cm"] = out
    return _CONST_CACHE["cm"]


def shard_tm(xfull):
    return [np.ascontiguousarray(xfull[c // 4, (c % 4) * T:(c % 4 + 1) * T]) for c in range(NC)]


def ln_all(xs, gamma_beta=None):
    if gamma_beta is None:
        nc = prog("ln", build_ln)
        r = launch(nc, [dict(x=xs[c]) for c in range(NC)])
    else:
        nc = prog("ln_gb", build_ln, gamma_beta=True)
        gb = np.zeros((2, D), np.float32)
        gb[0] = gamma_beta[0]
        gb[1] = gamma_beta[1]
        r = launch(nc, [dict(x=xs[c], gam=gb) for c in range(NC)])
    return ([r[c]["o"] for c in range(NC)], [r[c]["ob"].view(NPBF16) for c in range(NC)])


def lin_all(pname, xcs, w_t, K, M, **kw):
    nc = prog(pname, build_lin, pname, K, M, **{k: v for k, v in kw.items()
              if k not in ("b", "scol", "srow")})
    ims = []
    for c in range(NC):
        im = dict(x=xcs[c], w=w_t)
        for key in ("b", "scol"):
            if key in kw:
                im[key] = kw[key]
        if "srow" in kw:
            im["srow"] = kw["srow"][c] if isinstance(kw["srow"], list) else kw["srow"]
        ims.append(im)
    r = launch(nc, ims)
    return [r[c]["out"] for c in range(NC)]


def gather_fm_full(outs, b):
    return np.concatenate([ufm(outs[c]) for c in cores_of_b(b)], axis=1)


def layer0(xs, inp):
    w_kqv = np.asarray(inp["L0_attn_w_kqv"]).reshape(16, 64, 3, D)
    Wk = w_kqv[:, :, 0, :].reshape(D, D)
    Wq = w_kqv[:, :, 1, :].reshape(D, D)
    Wv = w_kqv[:, :, 2, :].reshape(D, D)
    xn, xnb = ln_all(xs)
    xnb_fm = [tfm(xnb[c].T) for c in range(NC)]
    q = lin_all("lin_kk", xnb_fm, tlhsT(Wq.T), D, D, x_in="bf16", out_dt="bf16")
    k = lin_all("lin_kk", xnb_fm, tlhsT(Wk.T), D, D, x_in="bf16", out_dt="bf16")
    v = lin_all("lin_kk", xnb_fm, tlhsT(Wv.T), D, D, x_in="bf16", out_dt="bf16")
    kf, va = [], []
    ones = np.ones((S, 1), NPBF16)
    for b in range(2):
        kfull = gather_fm_full([x.view(NPBF16) for x in k], b)
        kf.append(np.ascontiguousarray(kfull.reshape(8, 128, S)))
        vtm = gather_fm_full([x.view(NPBF16) for x in v], b).T
        cols = []
        for h in range(16):
            cols.append(vtm[:, h * 64:(h + 1) * 64])
            cols.append(ones)
            cols.append(ones)
        vaug = np.concatenate(cols, axis=1)
        va.append(np.ascontiguousarray(vaug.reshape(64, 128, 16 * 66)))
    masks = causal_masks()
    r = [None] * NC
    for pos in range(4):
        att = prog(f"att0_{pos}", build_att, f"att0_{pos}", 16, 64, 16,
                   causal=True, q0=pos * T)
        cores = [pos, pos + 4]
        res = run_bass_kernel_spmd(
            att,
            [dict(q=q[c].view(NPBF16), kf=kf[c // 4], va=va[c // 4],
                  maskq=masks[c][0], maskt=masks[c][1],
                  c_iden=np.eye(128, dtype=NPBF16), c_sel16=sel16()) for c in cores],
            core_ids=cores)
        for i, c in enumerate(cores):
            r[c] = res.results[i]
    ctx = [r[c]["out"].view(NPBF16) for c in range(NC)]
    s = lin_all("lin_kkb", ctx, tlhsT(np.asarray(inp["L0_attn_w_proj"]).T.copy()), D, D,
                x_in="bf16", out_dt="f32", bias=True, b=colM(inp["L0_attn_b_proj"]))
    rwT = np.zeros((D, 16), np.float32)
    rwT[:, 0:8] = np.asarray(inp["L0_ffn_router_w"]).T
    rr = launch(prog("router", build_router),
                [dict(x=xnb_fm[c], rw=tfm(rwT)) for c in range(NC)])
    ge_rows = []
    for c in range(NC):
        g = np.zeros((16, T), np.float32)
        g[0:8] = rr[c]["ge"].T
        ge_rows.append(g)
    wg = np.stack([tlhsT(np.asarray(inp["L0_ffn_w_gate"])[e]) for e in range(8)])
    wu = np.stack([tlhsT(np.asarray(inp["L0_ffn_w_up"])[e]) for e in range(8)])
    wd = np.stack([tlhsT(np.asarray(inp["L0_ffn_w_down"])[e]) for e in range(8)])
    rm = launch(prog("moe", build_moe),
                [dict(x=xnb_fm[c], ge=ge_rows[c], wg=wg, wu=wu, wd=wd, c_sel16=sel16())
                 for c in range(NC)])
    g = [rm[c]["g"] for c in range(NC)]
    x_fm = [tfm(xs[c].T) for c in range(NC)]
    rc = launch(prog("combine", build_elt_combine),
                [dict(x=x_fm[c], s=s[c], g=g[c]) for c in range(NC)])
    return [np.ascontiguousarray(ufm(rc[c]["o"]).T) for c in range(NC)]


def layer1(xs, inp):
    xn, xnb = ln_all(xs)
    xnb_fm = [tfm(xnb[c].T) for c in range(NC)]
    qg = lin_all("lin_1k4k", xnb_fm, tlhsT(np.asarray(inp["L1_attn_Wq"])), D, 4096,
                 x_in="bf16", out_dt="bf16")
    kv_k = lin_all("lin_1k512", xnb_fm, tlhsT(np.asarray(inp["L1_attn_Wk"])), D, 512,
                   x_in="bf16", out_dt="bf16")
    kv_v = lin_all("lin_1k512", xnb_fm, tlhsT(np.asarray(inp["L1_attn_Wv"])), D, 512,
                   x_in="bf16", out_dt="bf16")
    cosT, sinT = rope_tables()
    ropeq = prog("ropeq", build_rope, 8, 256)
    ropek = prog("ropek", build_rope, 2, 256)
    csl = [np.ascontiguousarray(cosT[:, (c % 4) * T:(c % 4 + 1) * T]) for c in range(NC)]
    ssl = [np.ascontiguousarray(sinT[:, (c % 4) * T:(c % 4 + 1) * T]) for c in range(NC)]
    rq = launch(ropeq, [dict(q=qg[c].view(NPBF16)[0:16], cos=csl[c], sin=ssl[c]) for c in range(NC)])
    rk = launch(ropek, [dict(q=kv_k[c].view(NPBF16), cos=csl[c], sin=ssl[c]) for c in range(NC)])
    qrope, krope = [], []
    for c in range(NC):
        qf = ufm(qg[c].view(NPBF16)[0:16]).copy()
        pq = rq[c]["o"].view(NPBF16)
        for h in range(8):
            qf[h * 256:h * 256 + 64] = pq[h]
        qrope.append(np.ascontiguousarray(qf.reshape(16, 128, T)))
        kfull = ufm(kv_k[c].view(NPBF16)).copy()
        pk = rk[c]["o"].view(NPBF16)
        for h in range(2):
            kfull[h * 256:h * 256 + 64] = pk[h]
        krope.append(kfull)
    kf, va = [], []
    ones = np.ones((S, 1), NPBF16)
    for b in range(2):
        kfull = np.concatenate([krope[c] for c in cores_of_b(b)], axis=1)
        kf.append(np.ascontiguousarray(kfull.reshape(4, 128, S)))
        vtm = gather_fm_full([x.view(NPBF16) for x in kv_v], b).T
        cols = []
        for kv in range(2):
            cols.append(vtm[:, kv * 256:(kv + 1) * 256])
            cols.append(ones)
            cols.append(ones)
        vaug = np.concatenate(cols, axis=1)
        va.append(np.ascontiguousarray(vaug.reshape(64, 128, 2 * 258)))
    att = prog("att1", build_att, "att1", 8, 256, 2, causal=False, gate=True)
    r = launch(att, [dict(q=qrope[c], kf=kf[c // 4], va=va[c // 4],
                          gt=np.ascontiguousarray(qg[c].view(NPBF16)[16:32]),
                          c_iden=np.eye(128, dtype=NPBF16), c_sel16=sel16())
                     for c in range(NC)])
    ctx = [r[c]["out"].view(NPBF16) for c in range(NC)]
    s = lin_all("lin_2k1k", ctx, tlhsT(np.asarray(inp["L1_attn_Wo"])), 2048, D,
                x_in="bf16", out_dt="f32")
    h = lin_all("lin_f8a", [tfm(xn[c].T) for c in range(NC)],
                tlhsT(np.asarray(inp["L1_ffn_W1"]).T.copy()), D, 4096,
                x_in="f32", cast="f8", epi="relu", out_dt="bf16",
                bias=True, b=colM(inp["L1_ffn_b1"]))
    y2 = lin_all("lin_f8b", [x.view(NPBF16) for x in h],
                 tlhsT(np.asarray(inp["L1_ffn_W2"]).T.copy()), 4096, D,
                 x_in="bf16", cast="f8", epi="copy", out_dt="bf16",
                 bias=True, b=colM(inp["L1_ffn_b2"]))
    dm = dropout_mask()
    dms = [tfm(dm[c // 4, (c % 4) * T:(c % 4 + 1) * T].T) for c in range(NC)]
    rg = launch(prog("dropg", build_elt_dropg),
                [dict(y=y2[c].view(NPBF16), m=dms[c]) for c in range(NC)])
    g = [rg[c]["g"] for c in range(NC)]
    x_fm = [tfm(xs[c].T) for c in range(NC)]
    rc = launch(prog("combine", build_elt_combine),
                [dict(x=x_fm[c], s=s[c], g=g[c]) for c in range(NC)])
    return [np.ascontiguousarray(ufm(rc[c]["o"]).T) for c in range(NC)]


def layer2(xs, inp):
    DI = 2048
    xn, xnb = ln_all(xs)
    xnb_fm = [tfm(xnb[c].T) for c in range(NC)]
    xn_fm = [tfm(xn[c].T) for c in range(NC)]
    ra = launch(prog("amax", build_amax), [dict(x=xn[c]) for c in range(NC)])
    gparts = np.zeros((2, 8), np.float32)
    for c in range(NC):
        gparts[0, c] = ra[c]["o"][0, 0]
    xz = lin_all("lin_1k4k", xnb_fm, tlhsT(np.asarray(inp["L2_attn_W_in"]).T.copy()),
                 D, 4096, x_in="bf16", out_dt="bf16")
    xssm = [np.ascontiguousarray(xz[c].view(NPBF16)[0:16]) for c in range(NC)]
    zb = [np.ascontiguousarray(xz[c].view(NPBF16)[16:32]) for c in range(NC)]
    dt = lin_all("lin_sp2k", xssm, tlhsT(np.asarray(inp["L2_attn_W_dt"]).T.copy()),
                 DI, DI, x_in="bf16", epi="softplus", out_dt="bf16",
                 bias=True, b=colM(inp["L2_attn_b_dt"]))
    wbc = np.zeros((DI, 128), np.float32)
    wbc[:, 0:16] = np.asarray(inp["L2_attn_W_B"]).T
    wbc[:, 16:32] = np.asarray(inp["L2_attn_W_C"]).T
    bc = lin_all("lin_bc2k", xssm, tlhsT(wbc), DI, 128, x_in="bf16", out_dt="f32")
    scan = prog("scan", build_scan, "scan")
    ims = []
    for c in range(NC):
        b = c // 4
        dsl = slice((c % 4) * 512, (c % 4 + 1) * 512)
        uf = np.concatenate([ufm(xssm[cc]) for cc in cores_of_b(b)], axis=1)[dsl]
        dtf = np.concatenate([ufm(dt[cc].view(NPBF16)) for cc in cores_of_b(b)], axis=1)[dsl]
        bcf = np.concatenate([ufm(bc[cc]) for cc in cores_of_b(b)], axis=1)
        brow = np.ascontiguousarray(bcf[0:16]); crow = np.ascontiguousarray(bcf[16:32])
        ims.append(dict(u=np.ascontiguousarray(uf.reshape(4, 128, S)),
                        dt=np.ascontiguousarray(dtf.reshape(4, 128, S)),
                        Brow=brow, Crow=crow,
                        Alog=np.ascontiguousarray(np.asarray(inp["L2_attn_A_log"])[dsl].reshape(4, 128, 16)),
                        Dcol=np.ascontiguousarray(np.asarray(inp["L2_attn_D"])[dsl].reshape(4, 128, 1)),
                        c_sel16=sel16()))
    rs = launch(scan, ims)
    y_fm = []
    for c in range(NC):
        b = c // 4
        yfull = np.concatenate([rs[cc]["y"].reshape(512, S) for cc in cores_of_b(b)], axis=0)
        y_fm.append(np.ascontiguousarray(yfull[:, (c % 4) * T:(c % 4 + 1) * T].reshape(16, 128, T)))
    rz = launch(prog("ysz", build_elt_ysz),
                [dict(y=y_fm[c], z=zb[c]) for c in range(NC)])
    ym = [rz[c]["o"].view(NPBF16) for c in range(NC)]
    rsg = launch(prog("sig1", build_sig1),
                 [dict(og=pad2(float(np.asarray(inp["L2_attn_output_gate"]).ravel()[0]))) for _ in range(NC)])
    sgrow = [np.broadcast_to(rsg[c]["o"].reshape(1, 1), (1, T)).astype(np.float32) for c in range(NC)]
    s = lin_all("lin_wo2k", ym, tlhsT(np.asarray(inp["L2_attn_W_out"]).T.copy()), DI, D,
                x_in="bf16", out_dt="f32", scale_row=True, srow=sgrow)
    wgu = np.concatenate([np.asarray(inp["L2_ffn_w_gate"]), np.asarray(inp["L2_ffn_w_up"])], axis=0)
    r1 = launch(prog("i8gu", build_lin_i8, "i8gu", D, 5632, silu_until=22),
                [dict(x=xn_fm[c], wrm=np.ascontiguousarray(wgu.reshape(44, 128, D)),
                      gparts=gparts, clip=pad2(float(np.asarray(inp["L2_ffn_clip_x"]).ravel()[0])),
                      c_iden=np.eye(128, dtype=NPBF16)) for c in range(NC)])
    gu = [r1[c]["out"] for c in range(NC)]
    rh = launch(prog("h2", build_elt_h2),
                [dict(g=np.ascontiguousarray(gu[c][0:22]), u=np.ascontiguousarray(gu[c][22:44]))
                 for c in range(NC)])
    hparts = np.zeros((2, 8), np.float32)
    for c in range(NC):
        hparts[0, c] = rh[c]["am"][0, 0]
    h2 = [rh[c]["h"] for c in range(NC)]
    g2 = launch(prog("i8dn", build_lin_i8, "i8dn", 2816, D, epi_all="silu"),
                [dict(x=h2[c], wrm=np.ascontiguousarray(np.asarray(inp["L2_ffn_w_down"]).reshape(8, 128, 2816)),
                      gparts=hparts, clip=pad2(float(np.asarray(inp["L2_ffn_clip_down"]).ravel()[0])),
                      c_iden=np.eye(128, dtype=NPBF16)) for c in range(NC)])
    g = [g2[c]["out"] for c in range(NC)]
    x_fm = [tfm(xs[c].T) for c in range(NC)]
    rc = launch(prog("combine", build_elt_combine),
                [dict(x=x_fm[c], s=s[c], g=g[c]) for c in range(NC)])
    return [np.ascontiguousarray(ufm(rc[c]["o"]).T) for c in range(NC)]


def layer3(xs, inp):
    DI = 2048
    xn, xnb = ln_all(xs)
    h3, h3b = ln_all(xn, gamma_beta=(np.asarray(inp["L3_attn_ln_gamma"]),
                                     np.asarray(inp["L3_attn_ln_beta"])))
    h3_fm = [tfm(h3b[c].T) for c in range(NC)]
    xz = lin_all("lin_1k4k", h3_fm, tlhsT(np.asarray(inp["L3_attn_in_proj_w"])),
                 D, 4096, x_in="bf16", out_dt="bf16")
    xi = [np.ascontiguousarray(xz[c].view(NPBF16)[0:16]) for c in range(NC)]
    z3 = [np.ascontiguousarray(xz[c].view(NPBF16)[16:32]) for c in range(NC)]
    conv = prog("conv3", build_conv3)
    ims = []
    for c in range(NC):
        b = c // 4
        dsl = slice((c % 4) * 512, (c % 4 + 1) * 512)
        xif = np.concatenate([ufm(xi[cc]) for cc in cores_of_b(b)], axis=1)[dsl]
        ims.append(dict(xi=np.ascontiguousarray(xif.reshape(4, 128, S)),
                        cw=np.ascontiguousarray(np.asarray(inp["L3_attn_conv_w"])[dsl].reshape(4, 128, 4)),
                        cb=np.ascontiguousarray(np.asarray(inp["L3_attn_conv_b"])[dsl].reshape(4, 128, 1))))
    rcv = launch(conv, ims)
    xi2_fm = []
    for c in range(NC):
        b = c // 4
        xif = np.concatenate([rcv[cc]["xo"].view(NPBF16).reshape(512, S) for cc in cores_of_b(b)], axis=0)
        xi2_fm.append(np.ascontiguousarray(xif[:, (c % 4) * T:(c % 4 + 1) * T].reshape(16, 128, T)))
    wxp = np.zeros((DI, 128), np.float32)
    wxp[:, 0:96] = np.asarray(inp["L3_attn_x_proj_w"])
    xdb = lin_all("lin_bc2k", xi2_fm, tlhsT(wxp), DI, 128, x_in="bf16", out_dt="f32")
    wdt = np.zeros((128, DI), np.float32)
    wdt[0:64] = np.asarray(inp["L3_attn_dt_proj_w"])
    dt3 = lin_all("lin_spdt", xdb, tlhsT(wdt), 128, DI, x_in="f32", epi="softplus",
                  out_dt="bf16", bias=True, b=colM(inp["L3_attn_dt_proj_b"]))
    scan = prog("scan", build_scan, "scan")
    ims = []
    for c in range(NC):
        b = c // 4
        dsl = slice((c % 4) * 512, (c % 4 + 1) * 512)
        dtf = np.concatenate([ufm(dt3[cc].view(NPBF16)) for cc in cores_of_b(b)], axis=1)[dsl]
        xdbf = np.concatenate([ufm(xdb[cc]) for cc in cores_of_b(b)], axis=1)
        brow = np.ascontiguousarray(xdbf[64:80]); crow = np.ascontiguousarray(xdbf[80:96])
        xif = np.concatenate([rcv[cc]["xo"].view(NPBF16).reshape(512, S) for cc in cores_of_b(b)], axis=0)[dsl]
        ims.append(dict(u=np.ascontiguousarray(xif.reshape(4, 128, S)),
                        dt=np.ascontiguousarray(dtf.reshape(4, 128, S)),
                        Brow=brow, Crow=crow,
                        Alog=np.ascontiguousarray(np.asarray(inp["L3_attn_A_log"])[dsl].reshape(4, 128, 16)),
                        Dcol=np.ascontiguousarray(np.asarray(inp["L3_attn_D"])[dsl].reshape(4, 128, 1)),
                        c_sel16=sel16()))
    rs = launch(scan, ims)
    y_fm = []
    for c in range(NC):
        b = c // 4
        yfull = np.concatenate([rs[cc]["y"].reshape(512, S) for cc in cores_of_b(b)], axis=0)
        y_fm.append(np.ascontiguousarray(yfull[:, (c % 4) * T:(c % 4 + 1) * T].reshape(16, 128, T)))
    rz = launch(prog("ysz", build_elt_ysz),
                [dict(y=y_fm[c], z=z3[c]) for c in range(NC)])
    ym = [rz[c]["o"].view(NPBF16) for c in range(NC)]
    hg = lin_all("lin_gel2k", ym, tlhsT(np.asarray(inp["L3_attn_out_proj_w"])), DI, D,
                 x_in="bf16", epi="gelu", out_dt="bf16")
    gg = lin_all("lin_glu", [x.view(NPBF16) for x in hg], tlhsT(np.asarray(inp["L3_attn_glu_w"])),
                 D, DI, x_in="bf16", out_dt="f32", bias=True, b=colM(inp["L3_attn_glu_b"]))
    rgs = launch(prog("glusk", build_elt_glu_skip),
                 [dict(gg=gg[c], sk=tfm(xn[c].T)) for c in range(NC)])
    s = [rgs[c]["o"] for c in range(NC)]
    nw1 = np.zeros((2, D), np.float32); nw1[0] = np.asarray(inp["L3_ffn_norm1_w"])
    rq1 = launch(prog("rmsq1", build_rmsq, "rmsq1", D),
                 [dict(x=xn[c], nw=nw1) for c in range(NC)])
    asab = pad2(float(np.asarray(inp["L3_ffn_act_scale"]).ravel()[0]),
                float(np.asarray(inp["L3_ffn_act_bias"]).ravel()[0]))
    r1 = launch(prog("tern1", build_lin_tern, "tern1", D, 4096, post="h3"),
                [dict(x=tfm(rq1[c]["xq"].view(NPBF16).T), wrm=np.ascontiguousarray(np.asarray(inp["L3_ffn_w1"]).reshape(32, 128, D)),
                      srow=rq1[c]["srow"], asab=asab, c_iden=np.eye(128, dtype=NPBF16))
                 for c in range(NC)])
    h3f = [np.ascontiguousarray(ufm(r1[c]["out"]).T) for c in range(NC)]
    nw2 = np.zeros((2, 4096), np.float32); nw2[0] = np.asarray(inp["L3_ffn_norm2_w"])
    rq2 = launch(prog("rmsq2", build_rmsq, "rmsq2", 4096),
                 [dict(x=h3f[c], nw=nw2) for c in range(NC)])
    r2 = launch(prog("tern2", build_lin_tern, "tern2", 4096, D, post="silu"),
                [dict(x=tfm(rq2[c]["xq"].view(NPBF16).T), wrm=np.ascontiguousarray(np.asarray(inp["L3_ffn_w2"]).reshape(8, 128, 4096)),
                      srow=rq2[c]["srow"], c_iden=np.eye(128, dtype=NPBF16))
                 for c in range(NC)])
    g = [r2[c]["out"] for c in range(NC)]
    x_fm = [tfm(xs[c].T) for c in range(NC)]
    rc = launch(prog("combine", build_elt_combine),
                [dict(x=x_fm[c], s=s[c], g=g[c]) for c in range(NC)])
    return [np.ascontiguousarray(ufm(rc[c]["o"]).T) for c in range(NC)]


def kernel(**inp):
    x = np.asarray(inp["L0_attn_x"], np.float32)
    xs = shard_tm(x)
    xs = layer0(xs, inp)
    xs = layer1(xs, inp)
    xs = layer2(xs, inp)
    xs = layer3(xs, inp)
    out = np.empty((2, S, D), np.float32)
    for c in range(NC):
        out[c // 4, (c % 4) * T:(c % 4 + 1) * T] = xs[c]
    return out


# version 06
# speedup vs baseline: 1.2736x; 1.2736x over previous
import ml_dtypes
"""Raw-Bass builder with automatic semaphore dependency tracking.

Records ops per engine with read/write sets (buffer = tensor name), computes
required semaphore waits (vector-clock style, transitive), and emits the
program with explicit wait_ge / then_inc on per-engine semaphores.

Engines: sync, vector, scalar, tensor, gpsimd. Each engine has a compute
semaphore C_<eng> (+1 per op) and a DMA semaphore D_<eng> (+16 per dma).
Collectives use sem CC (+1 each), issued on gpsimd.
"""
import numpy as np
from contextlib import ExitStack
import concourse.bass as bass
import concourse.mybir as mybir

F32 = mybir.dt.float32
BF16 = mybir.dt.bfloat16
F8E5 = mybir.dt.float8e5
I32 = mybir.dt.int32
AF = mybir.ActivationFunctionType
ALU = mybir.AluOpType
AX = mybir.AxisListType

ENGINES = ["sync", "vector", "scalar", "tensor", "gpsimd"]


def _names(aps):
    out = []
    for ap in aps:
        if ap is None:
            continue
        t = getattr(ap, "tensor", None)
        n = getattr(t, "name", None)
        if n is None:
            raise ValueError(f"cannot extract tensor name from {ap}")
        out.append(n)
    return out


class Prog:
    def __init__(self, name="prog"):
        self.nc = bass.Bass("TRN2", target_bir_lowering=False)
        self.es = ExitStack()
        self.recs = {e: [] for e in ENGINES}
        self.sem_names = [f"C_{e}" for e in ENGINES] + [f"D_{e}" for e in ENGINES] + ["CC"]
        self.cnt = {s: 0 for s in self.sem_names}
        self.clock = {e: {} for e in ENGINES}
        self.last_write = {}
        self.readers = {}
        self.dma_gate = {}
        self._finalized = False
        self._n_ops = 0

    def din(self, name, shape, dtype):
        return self.nc.dram_tensor(name, list(shape), dtype, kind="ExternalInput").ap()

    def dout(self, name, shape, dtype):
        return self.nc.dram_tensor(name, list(shape), dtype, kind="ExternalOutput").ap()

    def dint(self, name, shape, dtype):
        return self.nc.dram_tensor(name, list(shape), dtype, kind="Internal").ap()

    def sb(self, name, shape, dtype):
        return self.es.enter_context(self.nc.sbuf_tensor(name, list(shape), dtype))

    def ps(self, name, shape, dtype=F32):
        return self.es.enter_context(self.nc.psum_tensor(name, list(shape), dtype))

    def _need(self, eng, reads, writes):
        need = {}
        def add(sem, val, pclock):
            cur = need.get(sem)
            if cur is None or val > cur[0]:
                need[sem] = (val, pclock)
        for r in reads:
            lw = self.last_write.get(r)
            if lw is not None:
                add(lw[0], lw[1], lw[2])
        for w in writes:
            lw = self.last_write.get(w)
            if lw is not None:
                add(lw[0], lw[1], lw[2])
            for sem, (val, rclock) in self.readers.get(w, {}).items():
                add(sem, val, rclock)
        clk = self.clock[eng]
        waits = {}
        for sem, (val, pclock) in need.items():
            if clk.get(sem, 0) >= val:
                continue
            waits[sem] = val
            clk[sem] = val
            if pclock:
                for s2, v2 in pclock.items():
                    if clk.get(s2, 0) < v2:
                        clk[s2] = v2
        return waits

    def op(self, eng, emit, reads=(), writes=(), dma=False, collective=False, tag=""):
        rnames = [r if isinstance(r, str) else _names([r])[0] for r in reads if r is not None]
        wnames = [w if isinstance(w, str) else _names([w])[0] for w in writes if w is not None]
        waits = self._need(eng, rnames, wnames)
        sem = "CC" if collective else (f"D_{eng}" if dma else f"C_{eng}")
        amt = 1 if collective else (16 if dma else 1)
        unordered = sem.startswith("D_") or sem == "CC"
        if unordered:
            gate = self.dma_gate.get(sem)
            if gate:
                clk = self.clock[eng]
                extra = {}
                for s2, v2 in gate.items():
                    if s2 != sem and clk.get(s2, 0) < v2:
                        extra[s2] = v2
                        clk[s2] = v2
                for s2, v2 in extra.items():
                    waits[s2] = max(waits.get(s2, 0), v2)
        for s2 in list(waits.keys()):
            if s2.startswith("D_") or s2 == "CC":
                waits[s2] = self.cnt[s2]
                self.clock[eng][s2] = self.cnt[s2]
        self.cnt[sem] += amt
        cafter = self.cnt[sem]
        for s2 in waits:
            if s2.startswith("D_") or s2 == "CC":
                g = self.dma_gate.setdefault(s2, {})
                g[sem] = max(g.get(sem, 0), cafter)
        myclock = dict(self.clock[eng])
        myclock[sem] = max(myclock.get(sem, 0), cafter)
        for w in wnames:
            self.last_write[w] = (sem, cafter, myclock)
            self.readers[w] = {}
        for r in rnames:
            d = self.readers.setdefault(r, {})
            old = d.get(sem)
            if old is None or cafter > old[0]:
                d[sem] = (cafter, myclock)
        self.recs[eng].append((waits, emit, sem, amt, tag))
        self._n_ops += 1
        return cafter

    def constf(self, value, dtype=F32):
        key = ("constf", float(value), dtype)
        cache = self.__dict__.setdefault("_consts", {})
        if key not in cache:
            tile = self.__dict__.setdefault("_const_tile_" + str(dtype), None)
            name = f"cst_{len(cache)}"
            t = self.sb(name, [128, 1], dtype)
            self.memset("vector", t[:], float(value), tag="const")
            cache[key] = t
        return cache[key][:]

    def bcrow(self, which):
        cache = self.__dict__.setdefault("_consts", {})
        key = ("bcrow", which)
        if key not in cache:
            t = self.sb(f"cbc_{which}", [2, 128], F32)
            self.memset("vector", t[:], 0.0, tag="const")
            row = 0 if which == "r0" else 1
            self.memset("vector", t[row:row + 1, :], 1.0, tag="const")
            cache[key] = t
        return cache[key][:]

    def ones2(self, dtype=BF16):
        cache = self.__dict__.setdefault("_consts", {})
        key = ("ones2", dtype)
        if key not in cache:
            t = self.sb(f"cones2_{str(dtype)[-4:]}", [128, 2], dtype)
            self.memset("vector", t[:], 1.0, tag="const")
            cache[key] = t
        return cache[key][:]

    def sel16(self):
        cache = self.__dict__.setdefault("_consts", {})
        if "sel16" not in cache:
            a = self.din("c_sel16", [16, 2048], F32)
            t = self.sb("s_c_sel16", [16, 2048], F32)
            self.dma("sync", t[:], a)
            cache["sel16"] = t
        return cache["sel16"][:]

    def iden(self):
        cache = self.__dict__.setdefault("_consts", {})
        if "iden" not in cache:
            a = self.din("c_iden", [128, 128], BF16)
            t = self.sb("s_c_iden", [128, 128], BF16)
            self.dma("sync", t[:], a)
            cache["iden"] = t
        return cache["iden"][:]

    def bcast_row(self, psum_slice, row_pair_ap, tag="bcast"):
        return self.mm(psum_slice, [(self.bcrow("r0"), row_pair_ap)], tag=tag)

    def dma(self, eng, out, in_, **kw):
        return self.op(eng, lambda e: e.dma_start(out=out, in_=in_, **kw),
                       reads=[in_], writes=[out], dma=True, tag="dma")

    def cc(self, kind, alu, groups, ins, outs, tag="cc"):
        return self.op("gpsimd",
                       lambda e: e.collective_compute(kind, alu, replica_groups=groups,
                                                      ins=[a.opt() for a in ins],
                                                      outs=[a.opt() for a in outs]),
                       reads=list(ins), writes=list(outs), collective=True, tag=tag)

    def mm(self, psum, pairs, reads=None, tag="mm"):
        n = len(pairs)
        def emit(e):
            last = None
            for i, (l, r) in enumerate(pairs):
                last = e.matmul(psum, l, r, start=(i == 0), stop=(i == n - 1))
            return last
        rd = reads if reads is not None else [l for l, r in pairs] + [r for l, r in pairs]
        return self.op("tensor", emit, reads=rd, writes=[psum], tag=tag)

    def mm_acc(self, psum, pairs, start, stop, tag="mm_acc"):
        n = len(pairs)
        def emit(e):
            last = None
            for i, (l, r) in enumerate(pairs):
                last = e.matmul(psum, l, r, start=(start and i == 0), stop=(stop and i == n - 1))
            return last
        rd = [l for l, r in pairs] + [r for l, r in pairs] + ([psum] if not start else [])
        return self.op("tensor", emit, reads=rd, writes=[psum], tag=tag)

    def mmflex(self, emitlist, reads, writes, tag="mmflex"):
        def emit(e):
            last = None
            for f in emitlist:
                last = f(e)
            return last
        return self.op("tensor", emit, reads=reads, writes=writes, tag=tag)

    def act(self, out, in_, func, scale=1.0, bias=0.0, accum_out=None, tag="act"):
        reads = [in_]
        if not isinstance(scale, (int, float)):
            reads.append(scale)
        if not isinstance(bias, (int, float)):
            reads.append(bias)
        kw = {}
        if accum_out is not None:
            kw["accum_out"] = accum_out
        return self.op("scalar",
                       lambda e: e.activation(out, in_, func, bias=bias, scale=scale, **kw),
                       reads=reads, writes=[out] + ([accum_out] if accum_out is not None else []),
                       tag=tag)

    def tt(self, eng, out, a, b, alu, tag="tt"):
        return self.op(eng, lambda e: e.tensor_tensor(out, a, b, op=alu),
                       reads=[a, b], writes=[out], tag=tag)

    def ts(self, eng, out, in_, s1, s2=None, op0=ALU.mult, op1=ALU.bypass, tag="ts"):
        reads = [in_] + [s for s in (s1, s2) if not isinstance(s, (int, float)) and s is not None]
        def emit(e):
            return e.tensor_scalar(out, in_, s1, s2, op0=op0, op1=op1)
        return self.op(eng, emit, reads=reads, writes=[out], tag=tag)

    def stt(self, eng, out, in0, scalar, in1, op0, op1, tag="stt"):
        reads = [in0, in1] + ([scalar] if not isinstance(scalar, (int, float)) else [])
        return self.op(eng, lambda e: e.scalar_tensor_tensor(out, in0, scalar, in1, op0=op0, op1=op1),
                       reads=reads, writes=[out], tag=tag)

    def copy(self, eng, out, in_, tag="copy"):
        return self.op(eng, lambda e: e.tensor_copy(out, in_), reads=[in_], writes=[out], tag=tag)

    def scopy(self, out, in_, tag="scopy"):
        return self.op("scalar", lambda e: e.copy(out, in_), reads=[in_], writes=[out], tag=tag)

    def red(self, eng, out, in_, alu, axis=AX.X, absval=None, tag="red"):
        def emit(e):
            return e.tensor_reduce(out, in_, axis=axis, op=alu, apply_absolute_value=absval)
        return self.op(eng, emit, reads=[in_], writes=[out], tag=tag)

    def scan(self, eng, out, a, b, initial, tag="scan"):
        reads = [a, b] + ([initial] if not isinstance(initial, (int, float)) else [])
        return self.op(eng, lambda e: e.tensor_tensor_scan(out, a, b, initial, op0=ALU.mult, op1=ALU.add),
                       reads=reads, writes=[out], tag=tag)

    def memset(self, eng, ap, val, tag="memset"):
        return self.op(eng, lambda e: e.memset(ap, val), writes=[ap], tag=tag)

    def transpose(self, psum_out, in_, identity, tag="tr"):
        return self.op("tensor", lambda e: e.transpose(psum_out, in_, identity),
                       reads=[in_, identity], writes=[psum_out], tag=tag)

    def finalize(self):
        assert not self._finalized
        self._finalized = True
        sems = {}
        for s in self.sem_names:
            if self.cnt[s] > 0 or s.startswith("C_"):
                sems[s] = self.es.enter_context(self.nc.semaphore(s))
        self._semh = sems
        block = self.es.enter_context(self.nc.Block())
        with self.es:
            for eng in ENGINES:
                recs = self.recs[eng]
                if not recs:
                    continue
                def make(eng, recs):
                    def body(proxy):
                        for waits, emit, sem, amt, tag in recs:
                            for s, v in sorted(waits.items()):
                                proxy.wait_ge(sems[s], v)
                            inst = emit(proxy)
                            inst.then_inc(sems[sem], amt)
                    return body
                getattr(block, eng)(make(eng, recs))
        return self.nc

    def final_waits(self, eng="gpsimd"):
        final = dict(self.cnt)
        self.recs[eng].append(({}, self._final_emit_factory(final, eng), f"C_{eng}", 1, "final"))
        self.cnt[f"C_{eng}"] += 1

    def _final_emit_factory(self, final, eng):
        def emit(proxy):
            for s, v in sorted(final.items()):
                if v > 0 and s != f"C_{eng}":
                    proxy.wait_ge(self._semh[s], v)
            return proxy.nop()
        return emit


"""Per-launch program builders for the mk_recurrent_d4 module (v1, unfused).

Conventions:
- token-major (tm): [tokens on partitions, features on free]; per-core T tokens.
- feature-major (fm): x stored in DRAM tiled as [K/128, 128, T].
- Weights for out = W @ x: host stages lhsT tiles [K/128, M/128, 128, 128] (= W.T tiled).
- Host does all relayouts (slice/transpose/retile) between launches.
"""


def cdiv(a, b):
    return (a + b - 1) // b


def tile_lhsT(wT):
    K, M = wT.shape
    out = np.empty((M // 128, 128, K), wT.dtype)
    for mt in range(M // 128):
        for kc in range(K // 128):
            out[mt, :, kc * 128:(kc + 1) * 128] = wT[kc * 128:(kc + 1) * 128,
                                                     mt * 128:(mt + 1) * 128]
    return np.ascontiguousarray(out)


def tile_fm(x):
    K, T = x.shape
    return np.ascontiguousarray(x.reshape(K // 128, 128, T))


def untile_fm(x):
    k, p, T = x.shape
    return x.reshape(k * p, T)


def build_ln(T=2048, D=1024, eps=1e-5, gamma_beta=False, name="ln"):
    P = Prog(name)
    x = P.din("x", [T, D], F32)
    o = P.dout("o", [T, D], F32)
    ob = P.dout("ob", [T, D], BF16)
    if gamma_beta:
        gam = P.din("gam", [2, D], F32)
        s_g1 = P.sb("s_g1", [2, D], F32)
        s_g2 = P.sb("s_g2", [2, D], F32)
        P.memset("vector", s_g1[:], 0.0)
        P.memset("vector", s_g2[:], 0.0)
        P.dma("sync", s_g1[0:1, :], gam[0:1, :])
        P.dma("sync", s_g2[0:1, :], gam[1:2, :])
        s_gamr = P.sb("s_gamr", [128, D], F32)
        s_betr = P.sb("s_betr", [128, D], F32)
        p_b = P.ps("p_b", [128, 512])
        for j in range(D // 512):
            sl = slice(j * 512, (j + 1) * 512)
            P.mm(p_b[:], [(P.bcrow("r0"), s_g1[:, sl])], tag="bcg")
            P.scopy(s_gamr[:, sl], p_b[:])
            P.mm(p_b[:], [(P.bcrow("r0"), s_g2[:, sl])], tag="bcb")
            P.scopy(s_betr[:, sl], p_b[:])
    nt = T // 128
    s_x = P.sb("s_x", [128, 2 * D], F32)
    s_sq = P.sb("s_sq", [128, D], F32)
    s_c1 = P.sb("s_c1", [128, 1], F32)
    s_c2 = P.sb("s_c2", [128, 1], F32)
    s_c3 = P.sb("s_c3", [128, 1], F32)
    s_o = P.sb("s_o", [128, D], F32)
    s_ob = P.sb("s_ob", [128, D], BF16)
    for i in range(nt):
        r = slice(i * 128, (i + 1) * 128)
        xb = s_x[:, (i % 2) * D:(i % 2 + 1) * D]
        P.dma("sync", xb, x[r, :])
        P.act(s_sq[:], xb, AF.Square)
        P.red("vector", s_c1[:], xb, ALU.add)
        P.ts("vector", s_c1[:], s_c1[:], 1.0 / D, None, op0=ALU.mult)
        P.red("vector", s_c2[:], s_sq[:], ALU.add)
        P.ts("vector", s_c2[:], s_c2[:], 1.0 / D, None, op0=ALU.mult)
        P.tt("vector", s_c3[:], s_c1[:], s_c1[:], ALU.mult)
        P.tt("vector", s_c2[:], s_c2[:], s_c3[:], ALU.subtract)
        P.ts("vector", s_c2[:], s_c2[:], eps, None, op0=ALU.add)
        P.act(s_c2[:], s_c2[:], AF.Sqrt)
        P.op("vector", lambda e, a=s_c3, b=s_c2: e.reciprocal(a[:], b[:]),
             reads=[s_c2[:]], writes=[s_c3[:]])
        P.ts("vector", s_o[:], xb, s_c1[:], s_c3[:], op0=ALU.subtract, op1=ALU.mult)
        if gamma_beta:
            P.tt("vector", s_o[:], s_o[:], s_gamr[:], ALU.mult)
            P.tt("vector", s_o[:], s_o[:], s_betr[:], ALU.add)
        P.dma("gpsimd", o[r, :], s_o[:])
        P.copy("vector", s_ob[:], s_o[:])
        P.dma("gpsimd", ob[r, :], s_ob[:])
    P.final_waits("gpsimd")
    return P.finalize()


def build_lin(name, K, M, T=2048, *, x_in="bf16", cast="bf16",
              epi="copy", out_dt="f32", bias=False, scale=None,
              scale_col=False, scale_row=False, TCn=512):
    P = Prog(name)
    kc_n, mt_n, ntc = K // 128, M // 128, T // TCn
    xdt = {"f32": F32, "bf16": BF16, "f8": F8E5}[x_in]
    cdt = {"bf16": BF16, "f8": F8E5}[cast]
    odt = {"f32": F32, "bf16": BF16}[out_dt]
    x = P.din("x", [kc_n, 128, T], xdt)
    w = P.din("w", [mt_n, 128, K], F32)
    out = P.dout("out", [mt_n, 128, T], odt)
    epi_f = {"copy": AF.Identity, "relu": AF.Relu, "silu": AF.Silu, "softplus": None,
             "sigmoid": AF.Sigmoid, "exp": AF.Exp, "gelu": AF.Gelu}[epi]

    if xdt != cdt:
        s_xst = P.sb("s_xst", [128, 2 * T], xdt)
        s_xc = P.sb("s_xc", [128, kc_n * T], cdt)
        for kc in range(kc_n):
            st_ = s_xst[:, (kc % 2) * T:(kc % 2 + 1) * T]
            P.dma("sync", st_, x[kc])
            P.copy("vector", s_xc[:, kc * T:(kc + 1) * T], st_)
    else:
        s_xc = P.sb("s_x", [128, kc_n * T], xdt)
        for kc in range(kc_n):
            P.dma("sync", s_xc[:, kc * T:(kc + 1) * T], x[kc])

    def xt(kc, tc):
        return s_xc[:, kc * T + tc * TCn: kc * T + (tc + 1) * TCn]

    if bias:
        b = P.din("b", [mt_n, 128, 1], F32)
        s_b = P.sb("s_b", [128, mt_n], F32)
        for mt in range(mt_n):
            P.dma("sync", s_b[:, mt:mt + 1], b[mt])
        if cast == "f8":
            s_bq = P.sb("s_bq", [128, mt_n], F8E5)
            s_bf = P.sb("s_bf", [128, mt_n], F32)
            P.copy("vector", s_bq[:], s_b[:])
            P.copy("vector", s_bf[:], s_bq[:])
            s_b = s_bf
    if scale_col:
        scol = P.din("scol", [mt_n, 128, 1], F32)
        s_scol = P.sb("s_scol", [128, mt_n], F32)
        for mt in range(mt_n):
            P.dma("sync", s_scol[:, mt:mt + 1], scol[mt])
    if scale_row:
        srow = P.din("srow", [1, T], F32)
        s_srow = P.sb("s_srow", [2, T], F32)
        P.memset("vector", s_srow[:], 0.0)
        P.dma("sync", s_srow[0:1, :], srow)
        p_srow = P.ps("p_srow", [128, TCn])
        s_srowr = P.sb("s_srowr", [128, T], F32)
        for tc in range(ntc):
            sl = slice(tc * TCn, (tc + 1) * TCn)
            P.bcast_row(p_srow[:], s_srow[:, sl])
            P.scopy(s_srowr[:, sl], p_srow[:])

    WB = kc_n * 128
    s_wst = P.sb("s_wst", [128, 2 * WB], F32)
    s_wc = P.sb("s_wc", [128, 2 * WB], cdt)
    p_o = P.ps("p_o", [128, 2 * TCn])
    s_o = P.sb("s_o", [128, TCn], odt)
    if epi == "softplus":
        s_sp1 = P.sb("s_sp1", [128, TCn], F32)
        s_sp2 = P.sb("s_sp2", [128, TCn], F32)
    for mt in range(mt_n):
        wb = (mt % 2) * WB
        P.dma("sync", s_wst[:, wb:wb + WB], w[mt])
        for kc in range(kc_n):
            P.copy("vector", s_wc[:, wb + kc * 128:wb + (kc + 1) * 128],
                   s_wst[:, wb + kc * 128:wb + (kc + 1) * 128])
        for tc in range(ntc):
            po = p_o[:, (tc % 2) * TCn:(tc % 2 + 1) * TCn]
            P.mm(po, [(s_wc[:, wb + kc * 128:wb + (kc + 1) * 128], xt(kc, tc))
                      for kc in range(kc_n)], tag=f"lin{mt}")
            sc = s_scol[:, mt:mt + 1] if scale_col else (P.constf(scale) if scale else 1.0)
            bi = s_b[:, mt:mt + 1] if bias else 0.0
            if epi == "softplus":
                P.act(s_sp1[:], po, AF.Identity, bias=bi)
                P.ts("vector", s_sp2[:], s_sp1[:], -1.0, None, op0=ALU.mult)
                P.tt("vector", s_sp2[:], s_sp2[:], s_sp1[:], ALU.max)
                P.act(s_sp2[:], s_sp2[:], AF.Exp, scale=P.constf(-1.0))
                P.act(s_sp2[:], s_sp2[:], AF.Ln, bias=P.constf(1.0))
                P.ts("vector", s_sp1[:], s_sp1[:], 0.0, None, op0=ALU.max)
                P.tt("vector", s_o[:], s_sp1[:], s_sp2[:], ALU.add)
            else:
                P.act(s_o[:], po, epi_f, scale=sc, bias=bi)
            if scale_row:
                P.tt("vector", s_o[:], s_o[:], s_srowr[:, tc * TCn:(tc + 1) * TCn], ALU.mult)
            P.dma("gpsimd", out[mt, :, tc * TCn:(tc + 1) * TCn], s_o[:])
    P.final_waits("gpsimd")
    return P.finalize()


def build_amax(T=2048, D=1024, name="amax"):
    P = Prog(name)
    x = P.din("x", [T, D], F32)
    o = P.dout("o", [1, 1], F32)
    s_x = P.sb("s_x", [128, 2 * D], F32)
    s_m = P.sb("s_m", [128, 1], F32)
    s_acc = P.sb("s_acc", [128, 1], F32)
    s_o = P.sb("s_o", [1, 2], F32)
    P.memset("vector", s_acc[:], 0.0)
    for i in range(T // 128):
        xb = s_x[:, (i % 2) * D:(i % 2 + 1) * D]
        P.dma("sync", xb, x[i * 128:(i + 1) * 128, :])
        P.red("vector", s_m[:], xb, ALU.max, absval=True)
        P.tt("vector", s_acc[:], s_acc[:], s_m[:], ALU.max)
    P.red("gpsimd", s_o[0:1, 0:1], s_acc[:], ALU.max, axis=AX.C)
    P.dma("gpsimd", o, s_o[0:1, 0:1])
    P.final_waits("gpsimd")
    return P.finalize()


def build_rope(H, dh, T=2048, RD=64, name="rope"):
    P = Prog(name)
    qn = H * dh // 128
    q = P.din("q", [qn, 128, T], BF16)
    cos = P.din("cos", [RD, T], F32)
    sin = P.din("sin", [RD, T], F32)
    o = P.dout("o", [H, RD, T], BF16)
    s_q = P.sb("s_q", [128, qn * T], BF16)
    for i in range(qn):
        P.dma("sync", s_q[:, i * T:(i + 1) * T], q[i])
    s_cs = P.sb("s_cs", [RD, 2 * T], F32)
    P.dma("sync", s_cs[:, 0:T], cos)
    P.dma("sync", s_cs[:, T:2 * T], sin)
    TCn = 512
    s_x = P.sb("s_xr", [RD, TCn], F32)
    s_rot = P.sb("s_rot", [RD, TCn], F32)
    s_t1 = P.sb("s_t1", [RD, TCn], F32)
    s_ob = P.sb("s_obr", [RD, TCn], BF16)
    hf = RD // 2
    for h in range(H):
        base_tile = (h * dh) // 128
        off = (h * dh) % 128
        for tc in range(T // TCn):
            src = s_q[off:off + RD, base_tile * T + tc * TCn: base_tile * T + (tc + 1) * TCn]
            cosr = s_cs[:, tc * TCn:(tc + 1) * TCn]
            sinr = s_cs[:, T + tc * TCn:T + (tc + 1) * TCn]
            P.copy("vector", s_x[:], src)
            P.ts("vector", s_rot[0:hf, :], s_x[hf:RD, :], -1.0, None, op0=ALU.mult)
            P.copy("vector", s_rot[hf:RD, :], s_x[0:hf, :])
            P.tt("vector", s_t1[:], s_x[:], cosr, ALU.mult)
            P.tt("vector", s_rot[:], s_rot[:], sinr, ALU.mult)
            P.tt("vector", s_t1[:], s_t1[:], s_rot[:], ALU.add)
            P.copy("vector", s_ob[:], s_t1[:])
            P.dma("gpsimd", o[h, :, tc * TCn:(tc + 1) * TCn], s_ob[:])
    P.final_waits("gpsimd")
    return P.finalize()


def build_att(name, H, dh, kvH, T=2048, S=8192, causal=False, gate=False, dbg=False, q0=None):
    P = Prog(name)
    qn = H * dh // 128
    G = H // kvH
    AUG = dh + 2
    nkb = S // 128
    scale = 1.0 / float(np.sqrt(dh))
    q = P.din("q", [qn, 128, T], BF16)
    kf = P.din("kf", [kvH * dh // 128, 128, S], BF16)
    va = P.din("va", [nkb, 128, kvH * AUG], BF16)
    if causal:
        maskq = P.din("maskq", [T // 128, S // 512, 128, 512], BF16)
        maskt = P.din("maskt", [T // 512, nkb, 128, 512], BF16)
    if gate:
        gt = P.din("gt", [qn, 128, T], BF16)
    out = P.dout("out", [qn, 128, T], BF16)

    s_q = P.sb("s_q", [128, qn * T], BF16)
    for i in range(qn):
        P.dma("sync", s_q[:, i * T:(i + 1) * T], q[i])
    ktiles = dh // 128 if dh >= 128 else 1
    s_kw = P.sb("s_kw", [128, max(2, ktiles) * S], BF16)
    s_vh = P.sb("s_vh", [128, nkb * AUG], BF16)
    s_mq = P.sb("s_mq", [128, 2 * 512], BF16) if causal else None
    s_mt = P.sb("s_mt", [128, 2 * 512], BF16) if causal else None
    s_sc = P.sb("s_sc", [128, S], F32)
    s_mcol = P.sb("s_mcol", [128, T // 128], F32)
    s_mcolb = P.sb("s_mcolb", [128, T // 128], BF16)
    s_mrowT = P.sb("s_mrowT", [T // 128, 128], F32)
    p_mt = P.ps("p_mt", [T // 128, 128], BF16)
    s_mr = P.sb("s_mr", [128, 512], F32)
    s_e = P.sb("s_e", [128, 512], F32)
    s_p = P.sb("s_p", [128, 2 * 512], BF16)
    s_lst = P.sb("s_lst", [2, 512], F32)
    s_linv = P.sb("s_linv", [2, 512], F32)
    s_lr = P.sb("s_lr", [128, 512], F32)
    s_gtb = P.sb("s_gtb", [128, 512], BF16) if gate else None
    s_gts = P.sb("s_gts", [128, 512], F32) if gate else None
    s_ctx = P.sb("s_ctx", [128, qn * 512], BF16)
    p_s = P.ps("p_s", [128, 2 * 512])
    nctx = 1 if dh == 64 else 2
    p_ctx = [P.ps(f"p_ctx{i}", [128, 512]) for i in range(nctx)]
    p_l = P.ps("p_l", [2, 512]) if dh != 64 else None
    P.memset("vector", s_linv[:], 0.0)

    def qsl(h, r0, r1, t0, t1):
        a = h * dh + r0
        tile, off = a // 128, a % 128
        return s_q[off:off + (r1 - r0), tile * T + t0: tile * T + t1]

    def ksl(r0, r1, t0, t1, kvh):
        off = r0 % 128
        return s_kw[off:off + (r1 - r0), t0:t1]

    for h in range(H):
        kvh = h // G
        if dh == 64:
            P.dma("sync", s_kw[:, 0:S], kf[(kvh * dh) // 128])
            for kb in range(nkb):
                P.dma("sync", s_vh[:, kb * AUG:(kb + 1) * AUG],
                      va[kb][:, kvh * AUG:(kvh + 1) * AUG])
        elif h % G == 0:
            for j in range(ktiles):
                P.dma("sync", s_kw[:, j * S:(j + 1) * S], kf[kvh * ktiles + j])
            for kb in range(nkb):
                P.dma("sync", s_vh[:, kb * AUG:(kb + 1) * AUG],
                      va[kb][:, kvh * AUG:(kvh + 1) * AUG])
        koff = (kvh * dh) % 128 if dh == 64 else 0
        for qt in range(T // 128):
            if causal and q0 is not None:
                kb1_hi = min(S // 512, (q0 + qt * 128 + 127) // 512 + 1)
                diag1_lo = (q0 + qt * 128) // 512
            else:
                kb1_hi = S // 512
                diag1_lo = None
            for kb in range(kb1_hi):
                ps = p_s[:, (kb % 2) * 512:(kb % 2 + 1) * 512]
                if dh == 64:
                    pairs = [(qsl(h, 0, 64, qt * 128, (qt + 1) * 128),
                              ksl(koff, koff + 64, kb * 512, (kb + 1) * 512, kvh))]
                else:
                    pairs = [(qsl(h, j * 128, (j + 1) * 128, qt * 128, (qt + 1) * 128),
                              s_kw[:, j * S + kb * 512: j * S + (kb + 1) * 512])
                             for j in range(ktiles)]
                P.mm(ps, pairs, tag="qk1")
                dst = s_sc[:, kb * 512:(kb + 1) * 512]
                if causal and (diag1_lo is None or kb >= diag1_lo):
                    mb = s_mq[:, (kb % 2) * 512:(kb % 2 + 1) * 512]
                    P.dma("sync", mb, maskq[qt, kb])
                    P.tt("vector", dst, ps, mb, ALU.add)
                else:
                    P.scopy(dst, ps)
            P.red("vector", s_mcol[:, qt:qt + 1], s_sc[:, 0:kb1_hi * 512], ALU.max)
        P.ts("vector", s_mcolb[:], s_mcol[:], -scale, None, op0=ALU.mult)
        P.transpose(p_mt[:], s_mcolb[:], P.iden())
        P.scopy(s_mrowT[:], p_mt[:])
        for qc in range(T // 512):
            for j in range(4):
                qt = qc * 4 + j
                P.mm(p_s[:, j * 128:(j + 1) * 128],
                     [(P.sel16()[0:T // 128, qt * 128:(qt + 1) * 128], s_mrowT[:])], tag="bcm")
            P.scopy(s_mr[:], p_s[:, 0:512])
            if causal and q0 is not None:
                kb2_hi = min(nkb, (q0 + qc * 512 + 511) // 128 + 1)
                diag2_lo = (q0 + qc * 512) // 128
            else:
                kb2_hi = nkb
                diag2_lo = None
            for kb in range(kb2_hi):
                ps = p_s[:, 512:1024]
                if dh == 64:
                    pairs = [(ksl(koff, koff + 64, kb * 128, (kb + 1) * 128, kvh),
                              qsl(h, 0, 64, qc * 512, (qc + 1) * 512))]
                else:
                    pairs = [(s_kw[:, j * S + kb * 128: j * S + kb * 128 + 128],
                              qsl(h, j * 128, (j + 1) * 128, qc * 512, (qc + 1) * 512))
                             for j in range(ktiles)]
                P.mm(ps, pairs, tag="qk2")
                if causal and (diag2_lo is None or kb >= diag2_lo):
                    mb = s_mt[:, (kb % 2) * 512:(kb % 2 + 1) * 512]
                    P.dma("sync", mb, maskt[qc, kb])
                    P.tt("vector", ps, ps, mb, ALU.add)
                P.stt("vector", s_e[:], ps, scale, s_mr[:], op0=ALU.mult, op1=ALU.add)
                pb = s_p[:, (kb % 2) * 512:(kb % 2 + 1) * 512]
                P.act(pb, s_e[:], AF.Exp)
                st = (kb == 0)
                sp = (kb == kb2_hi - 1)
                vb = kb * AUG
                if dh == 64:
                    P.mm_acc(p_ctx[0][0:AUG, :], [(s_vh[:, vb:vb + AUG], pb)],
                             start=st, stop=sp, tag="av")
                else:
                    P.mm_acc(p_ctx[0][:, :], [(s_vh[:, vb:vb + 128], pb)],
                             start=st, stop=sp, tag="av")
                    P.mm_acc(p_ctx[1][:, :], [(s_vh[:, vb + 128:vb + 256], pb)],
                             start=st, stop=sp, tag="av2")
                    P.mm_acc(p_l[:, :], [(s_vh[:, vb + 256:vb + 258], pb)],
                             start=st, stop=sp, tag="avl")
            qt0 = (h * dh) // 128
            P.scopy(s_lst[0:2, :], p_ctx[0][64:66, :] if dh == 64 else p_l[0:2, :])
            P.op("vector", lambda e: e.reciprocal(s_linv[0:1, :], s_lst[0:1, :]),
                 reads=[s_lst[:]], writes=[s_linv[:]], tag="recip")
            P.mm(p_s[:, 0:512], [(P.bcrow("r0"), s_linv[:])], tag="bcl")
            P.scopy(s_lr[:], p_s[:, 0:512])
            qs = slice(qc * 512, (qc + 1) * 512)
            if dh == 64:
                off = (h * dh) % 128
                dst = s_ctx[off:off + 64, qt0 * 512:(qt0 + 1) * 512]
                P.tt("vector", dst, p_ctx[0][0:64, :], s_lr[0:64, :], ALU.mult)
            else:
                for j in range(2):
                    dst = s_ctx[:, (qt0 + j) * 512:(qt0 + j + 1) * 512]
                    P.tt("vector", dst, p_ctx[j][:, :], s_lr[:], ALU.mult)
                    if gate:
                        P.dma("sync", s_gtb[:], gt[qt0 + j][:, qs])
                        P.act(s_gts[:], s_gtb[:], AF.Silu)
                        P.tt("vector", dst, dst, s_gts[:], ALU.mult)
            if dh == 64:
                off = (h * dh) % 128
                P.dma("gpsimd", out[qt0][off:off + 64, qs],
                      s_ctx[off:off + 64, qt0 * 512:(qt0 + 1) * 512])
            else:
                for j in range(2):
                    P.dma("gpsimd", out[qt0 + j][:, qs],
                          s_ctx[:, (qt0 + j) * 512:(qt0 + j + 1) * 512])
    P.final_waits("gpsimd")
    return P.finalize()


def build_scan(name, DC=512, S=8192, N=16, CH=512):
    P = Prog(name)
    ndt = DC // 128
    nch = S // CH
    u = P.din("u", [ndt, 128, S], BF16)
    dt = P.din("dt", [ndt, 128, S], BF16)
    Brow = P.din("Brow", [16, S], F32)
    Crow = P.din("Crow", [16, S], F32)
    Alog = P.din("Alog", [ndt, 128, N], F32)
    Dcol = P.din("Dcol", [ndt, 128, 1], F32)
    y = P.dout("y", [ndt, 128, S], F32)

    s_B = P.sb("s_B", [16, S], F32)
    s_C = P.sb("s_C", [16, S], F32)
    P.dma("sync", s_B[:], Brow)
    P.dma("sync", s_C[:], Crow)
    s_A = P.sb("s_A", [128, ndt * N], F32)
    s_D = P.sb("s_D", [128, ndt], F32)
    for d in range(ndt):
        P.dma("sync", s_A[:, d * N:(d + 1) * N], Alog[d])
        P.dma("sync", s_D[:, d:d + 1], Dcol[d])
    P.act(s_A[:], s_A[:], AF.Exp)
    P.ts("vector", s_A[:], s_A[:], -1.0, None, op0=ALU.mult)
    s_carry = P.sb("s_carry", [128, ndt * N], F32)
    P.memset("vector", s_carry[:], 0.0)

    s_u = P.sb("s_u", [128, 2 * S], BF16)
    s_dt = P.sb("s_dt", [128, 2 * S], BF16)
    s_w = P.sb("s_w", [128, CH], F32)
    s_a = P.sb("s_a", [128, CH], F32)
    s_b = P.sb("s_b", [128, CH], F32)
    s_h = P.sb("s_h", [128, CH], F32)
    s_pst = P.sb("s_pst", [128, CH * N], F32)
    s_y = P.sb("s_y", [128, CH], F32)
    p_bc = P.ps("p_bc", [128, 2 * CH])
    pst3 = s_pst[:].rearrange("p (t n) -> p t n", n=N)

    for d in range(ndt):
        db = (d % 2) * S
        P.dma("sync", s_u[:, db:db + S], u[d])
        P.dma("sync", s_dt[:, db:db + S], dt[d])
        for ch in range(nch):
            cs = slice(db + ch * CH, db + (ch + 1) * CH)
            P.tt("vector", s_w[:], s_dt[:, cs], s_u[:, cs], ALU.mult)
            for n in range(N):
                P.mm(p_bc[:, 0:CH], [(P.sel16()[:, n * 128:(n + 1) * 128], s_B[:, ch * CH:(ch + 1) * CH])], tag="bcB")
                P.mm(p_bc[:, CH:2 * CH], [(P.sel16()[:, n * 128:(n + 1) * 128], s_C[:, ch * CH:(ch + 1) * CH])], tag="bcC")
                P.act(s_a[:], s_dt[:, cs], AF.Exp, scale=s_A[:, d * N + n:d * N + n + 1])
                P.tt("vector", s_b[:], s_w[:], p_bc[:, 0:CH], ALU.mult)
                P.scan("vector", s_h[:], s_a[:], s_b[:], s_carry[:, d * N + n:d * N + n + 1])
                P.copy("vector", s_carry[:, d * N + n:d * N + n + 1], s_h[:, CH - 1:CH])
                P.tt("vector", pst3[:, :, n], s_h[:], p_bc[:, CH:2 * CH], ALU.mult)
            P.red("vector", s_y[:], pst3, ALU.add, axis=AX.X)
            P.stt("vector", s_y[:], s_u[:, cs], s_D[:, d:d + 1], s_y[:], op0=ALU.mult, op1=ALU.add)
            P.dma("gpsimd", y[d][:, ch * CH:(ch + 1) * CH], s_y[:])
    P.final_waits("gpsimd")
    return P.finalize()


def build_conv3(DC=512, S=8192, KW=4, name="conv3"):
    P = Prog(name)
    ndt = DC // 128
    CH = 512
    xi = P.din("xi", [ndt, 128, S], BF16)
    cw = P.din("cw", [ndt, 128, KW], F32)
    cb = P.din("cb", [ndt, 128, 1], F32)
    xo = P.dout("xo", [ndt, 128, S], BF16)
    s_x = P.sb("s_x", [128, 2 * S], BF16)
    s_cw = P.sb("s_cw", [128, ndt * KW], F32)
    s_cb = P.sb("s_cb", [128, ndt], F32)
    s_acc = P.sb("s_acc", [128, CH], F32)
    s_o = P.sb("s_o", [128, CH], BF16)
    for d in range(ndt):
        P.dma("sync", s_cw[:, d * KW:(d + 1) * KW], cw[d])
        P.dma("sync", s_cb[:, d:d + 1], cb[d])
    for d in range(ndt):
        db = (d % 2) * S
        P.dma("sync", s_x[:, db:db + S], xi[d])
        for ch in range(S // CH):
            t0 = ch * CH
            P.memset("vector", s_acc[:], 0.0)
            for k in range(KW):
                sh = k - (KW - 1)
                lo = max(0, -(t0 + sh))
                src = s_x[:, db + t0 + sh + lo: db + t0 + sh + CH]
                P.stt("vector", s_acc[:, lo:CH], src, s_cw[:, d * KW + k:d * KW + k + 1],
                      s_acc[:, lo:CH], op0=ALU.mult, op1=ALU.add)
            P.act(s_o[:], s_acc[:], AF.Silu, bias=s_cb[:, d:d + 1])
            P.dma("gpsimd", xo[d][:, t0:t0 + CH], s_o[:])
    P.final_waits("gpsimd")
    return P.finalize()


def build_router(T=2048, D=1024, E=8, name="router"):
    P = Prog(name)
    kc_n = D // 128
    xn = P.din("x", [kc_n, 128, T], BF16)
    rw = P.din("rw", [kc_n, 128, 16], F32)
    ge_o = P.dout("ge", [T, E], F32)
    s_x = P.sb("s_x", [128, kc_n * T], BF16)
    for kc in range(kc_n):
        P.dma("sync", s_x[:, kc * T:(kc + 1) * T], xn[kc])
    s_rw = P.sb("s_rw", [128, kc_n * 16], F32)
    s_rwb = P.sb("s_rwb", [128, kc_n * 16], BF16)
    for kc in range(kc_n):
        P.dma("sync", s_rw[:, kc * 16:(kc + 1) * 16], rw[kc])
    P.copy("vector", s_rwb[:], s_rw[:])
    p_lg = P.ps("p_lg", [128, 16])
    s_lg = P.sb("s_lg", [128, 16], F32)
    s_m1 = P.sb("s_m1", [128, 1], F32)
    s_m2 = P.sb("s_m2", [128, 1], F32)
    s_sig = P.sb("s_sig", [128, 2], F32)
    s_eq1 = P.sb("s_eq1", [128, E], F32)
    s_eq2 = P.sb("s_eq2", [128, E], F32)
    s_msk = P.sb("s_msk", [128, E], F32)
    s_gs = P.sb("s_gs", [128, 1], F32)
    s_gi = P.sb("s_gi", [128, 1], F32)
    s_ge = P.sb("s_ge", [128, E], F32)
    for i in range(T // 128):
        P.mm(p_lg[:], [(s_x[:, kc * T + i * 128: kc * T + (i + 1) * 128],
                        s_rwb[:, kc * 16:(kc + 1) * 16]) for kc in range(kc_n)], tag="lg")
        P.scopy(s_lg[:], p_lg[:])
        lg = s_lg[:, 0:E]
        P.red("vector", s_m1[:], lg, ALU.max)
        P.ts("vector", s_eq1[:], lg, s_m1[:], None, op0=ALU.is_equal)
        P.stt("vector", s_msk[:], s_eq1[:], -1e30, lg, op0=ALU.mult, op1=ALU.add)
        P.red("vector", s_m2[:], s_msk[:], ALU.max)
        P.ts("vector", s_eq2[:], lg, s_m2[:], None, op0=ALU.is_equal)
        P.copy("vector", s_sig[:, 0:1], s_m1[:])
        P.copy("vector", s_sig[:, 1:2], s_m2[:])
        P.act(s_sig[:], s_sig[:], AF.Sigmoid)
        P.tt("vector", s_gs[:], s_sig[:, 0:1], s_sig[:, 1:2], ALU.add)
        P.ts("vector", s_gs[:], s_gs[:], 1e-10, None, op0=ALU.add)
        P.op("vector", lambda e: e.reciprocal(s_gi[:], s_gs[:]), reads=[s_gs[:]], writes=[s_gi[:]])
        P.ts("vector", s_eq1[:], s_eq1[:], s_sig[:, 0:1], s_gi[:], op0=ALU.mult, op1=ALU.mult)
        P.ts("vector", s_eq2[:], s_eq2[:], s_sig[:, 1:2], s_gi[:], op0=ALU.mult, op1=ALU.mult)
        P.tt("vector", s_ge[:], s_eq1[:], s_eq2[:], ALU.add)
        P.dma("gpsimd", ge_o[i * 128:(i + 1) * 128, :], s_ge[:])
    P.final_waits("gpsimd")
    return P.finalize()


def build_moe(T=2048, D=1024, HH=2048, E=8, name="moe"):
    P = Prog(name)
    kc_n = D // 128
    mt_n = HH // 128
    ot_n = D // 128
    TH = 1024
    x = P.din("x", [kc_n, 128, T], BF16)
    ge = P.din("ge", [16, T], F32)
    wg = P.din("wg", [E, mt_n, 128, D], F32)
    wu = P.din("wu", [E, mt_n, 128, D], F32)
    wd = P.din("wd", [E, ot_n, 128, HH], F32)
    g_o = P.dout("g", [ot_n, 128, T], F32)
    s_x = P.sb("s_x", [128, kc_n * T], BF16)
    for kc in range(kc_n):
        P.dma("sync", s_x[:, kc * T:(kc + 1) * T], x[kc])
    s_ge = P.sb("s_ge", [16, T], F32)
    P.dma("sync", s_ge[:], ge)
    s_wst = P.sb("s_wst", [128, 2 * D], F32)
    s_wgb = P.sb("s_wgb", [128, 2 * D], BF16)
    s_wust = P.sb("s_wust", [128, 2 * D], F32)
    s_wub = P.sb("s_wub", [128, 2 * D], BF16)
    s_wdst = P.sb("s_wdst", [128, 2 * HH], F32)
    s_wdb = P.sb("s_wdb", [128, 2 * HH], BF16)
    s_h = P.sb("s_h", [128, mt_n * 1024], BF16)
    s_sg = P.sb("s_sg", [128, 512], F32)
    s_ger = P.sb("s_ger", [128, 1024], F32)
    s_tmp = P.sb("s_tmp", [128, 512], F32)
    s_acc = P.sb("s_acc", [128, ot_n * TH], F32)
    s_go = P.sb("s_go", [128, 512], F32)
    p_g = P.ps("p_g", [128, 512])
    p_u = P.ps("p_u", [128, 512])
    p_d = P.ps("p_d", [128, 2 * 512])
    p_ger = P.ps("p_ger", [128, 512])

    def xt(kc, th, tc):
        t0 = th * TH + tc * 512
        return s_x[:, kc * T + t0: kc * T + t0 + 512]

    for th in range(T // TH):
        P.memset("vector", s_acc[:], 0.0)
        for e in range(E):
            for tc in range(TH // 512):
                t0 = th * TH + tc * 512
                P.mm(p_ger[:], [(P.sel16()[:, e * 128:(e + 1) * 128], s_ge[:, t0:t0 + 512])], tag="bcge")
                P.scopy(s_ger[:, tc * 512:(tc + 1) * 512], p_ger[:])
            for mt in range(mt_n):
                wb = (mt % 2) * D
                P.dma("sync", s_wst[:, wb:wb + D], wg[e, mt])
                P.copy("vector", s_wgb[:, wb:wb + D], s_wst[:, wb:wb + D])
                P.dma("sync", s_wust[:, wb:wb + D], wu[e, mt])
                P.copy("vector", s_wub[:, wb:wb + D], s_wust[:, wb:wb + D])
                for tc in range(TH // 512):
                    P.mm(p_g[:], [(s_wgb[:, wb + kc * 128:wb + (kc + 1) * 128], xt(kc, th, tc))
                                  for kc in range(kc_n)], tag="moeg")
                    P.mm(p_u[:], [(s_wub[:, wb + kc * 128:wb + (kc + 1) * 128], xt(kc, th, tc))
                                  for kc in range(kc_n)], tag="moeu")
                    P.act(s_sg[:], p_g[:], AF.Silu)
                    P.tt("vector", s_h[:, mt * TH + tc * 512: mt * TH + (tc + 1) * 512],
                         s_sg[:], p_u[:], ALU.mult)
            for ot in range(ot_n):
                wb = (ot % 2) * HH
                P.dma("sync", s_wdst[:, wb:wb + HH], wd[e, ot])
                P.copy("vector", s_wdb[:, wb:wb + HH], s_wdst[:, wb:wb + HH])
                for tc in range(TH // 512):
                    pd = p_d[:, (tc % 2) * 512:(tc % 2 + 1) * 512]
                    P.mm(pd, [(s_wdb[:, wb + ht * 128:wb + (ht + 1) * 128],
                               s_h[:, ht * TH + tc * 512: ht * TH + (tc + 1) * 512])
                              for ht in range(mt_n)], tag="moed")
                    P.tt("vector", s_tmp[:], pd, s_ger[:, tc * 512:(tc + 1) * 512], ALU.mult)
                    acs = s_acc[:, ot * TH + tc * 512: ot * TH + (tc + 1) * 512]
                    P.tt("vector", acs, acs, s_tmp[:], ALU.add)
        for ot in range(ot_n):
            for tc in range(TH // 512):
                acs = s_acc[:, ot * TH + tc * 512: ot * TH + (tc + 1) * 512]
                P.act(s_go[:], acs, AF.Silu)
                P.dma("gpsimd", g_o[ot][:, th * TH + tc * 512: th * TH + (tc + 1) * 512], s_go[:])
    P.final_waits("gpsimd")
    return P.finalize()


def build_lin_i8(name, K, M, T=2048, silu_until=None, epi_all="silu"):
    P = Prog(name)
    kc_n, mt_n, ntc = K // 128, M // 128, T // 512
    x = P.din("x", [kc_n, 128, T], F32)
    wrm = P.din("wrm", [mt_n, 128, K], F32)
    gparts = P.din("gparts", [2, 8], F32)
    clip = P.din("clip", [2, 2], F32)
    out = P.dout("out", [mt_n, 128, T], F32)

    s_gp = P.sb("s_gp", [2, 8], F32)
    s_cl = P.sb("s_cl", [2, 2], F32)
    s_s = P.sb("s_s", [2, 2], F32)
    P.dma("sync", s_gp[:], gparts)
    P.dma("sync", s_cl[:], clip)
    P.red("vector", s_s[:, 0:1], s_gp[:], ALU.max)
    P.act(s_cl[:], s_cl[:], AF.Sigmoid)
    P.tt("vector", s_s[:, 0:1], s_s[:, 0:1], s_cl[:, 0:1], ALU.mult)
    P.ts("vector", s_s[:, 0:1], s_s[:, 0:1], 1.0 / 127.0, None, op0=ALU.mult)
    p_bc = P.ps("p_bc", [128, 2])
    s_sx = P.sb("s_sx", [128, 1], F32)
    s_sxi = P.sb("s_sxi", [128, 1], F32)
    P.mm(p_bc[:], [(P.bcrow("r0"), s_s[:, 0:2])], tag="bcs")
    P.scopy(s_sx[:], p_bc[:, 0:1])
    P.op("vector", lambda e: e.reciprocal(s_sxi[:], s_sx[:]), reads=[s_sx[:]], writes=[s_sxi[:]])

    s_xst = P.sb("s_xst", [128, 2 * T], F32)
    s_xq = P.sb("s_xq", [128, kc_n * T], BF16)
    s_xi = P.sb("s_xi", [128, 512], I32)
    s_xf = P.sb("s_xf", [128, 512], F32)
    for kc in range(kc_n):
        stg = s_xst[:, (kc % 2) * T:(kc % 2 + 1) * T]
        P.dma("sync", stg, x[kc])
        for tc in range(ntc):
            sl = slice(kc * T + tc * 512, kc * T + (tc + 1) * 512)
            P.ts("vector", s_xf[:], stg[:, tc * 512:(tc + 1) * 512], s_sxi[:], None, op0=ALU.mult)
            P.copy("vector", s_xi[:], s_xf[:])
            P.copy("vector", s_xf[:], s_xi[:])
            P.ts("vector", s_xq[:, sl], s_xf[:], 127.0, -128.0, op0=ALU.min, op1=ALU.max)

    def xt(kc, tc):
        return s_xq[:, kc * T + tc * 512: kc * T + (tc + 1) * 512]

    s_wrm = P.sb("s_wrm", [128, K], F32)
    s_wq = P.sb("s_wq", [128, K], F32)
    s_wqb = P.sb("s_wqb", [128, K], BF16)
    s_wi = P.sb("s_wi", [128, 512], I32)
    s_wmx = P.sb("s_wmx", [128, 1], F32)
    s_wsi = P.sb("s_wsi", [128, 1], F32)
    s_oc = P.sb("s_oc", [128, 1], F32)
    s_lhsT = P.sb("s_lhsT", [128, 2 * K], BF16)
    p_tr = P.ps("p_tr", [128, 128], BF16)
    p_o = P.ps("p_o", [128, 2 * 512])
    s_o = P.sb("s_o", [128, 512], F32)
    for mt in range(mt_n):
        wb = (mt % 2) * K
        P.dma("sync", s_wrm[:], wrm[mt])
        P.red("vector", s_wmx[:], s_wrm[:], ALU.max, absval=True)
        P.ts("vector", s_wmx[:], s_wmx[:], 1.0 / 127.0, None, op0=ALU.mult)
        P.op("vector", lambda e: e.reciprocal(s_wsi[:], s_wmx[:]), reads=[s_wmx[:]], writes=[s_wsi[:]])
        P.tt("vector", s_oc[:], s_wmx[:], s_sx[:], ALU.mult)
        for j in range(K // 512):
            sl = slice(j * 512, (j + 1) * 512)
            P.ts("vector", s_wq[:, sl], s_wrm[:, sl], s_wsi[:], None, op0=ALU.mult)
            P.copy("vector", s_wi[:], s_wq[:, sl])
            P.copy("vector", s_wq[:, sl], s_wi[:])
        P.ts("vector", s_wqb[:], s_wq[:], 127.0, -128.0, op0=ALU.min, op1=ALU.max)
        for kc in range(kc_n):
            P.transpose(p_tr[:], s_wqb[:, kc * 128:(kc + 1) * 128], P.iden())
            P.scopy(s_lhsT[:, wb + kc * 128:wb + (kc + 1) * 128], p_tr[:])
        fn = AF.Silu if (silu_until is None and epi_all == "silu") or (silu_until is not None and mt < silu_until) else AF.Identity
        for tc in range(ntc):
            po = p_o[:, (tc % 2) * 512:(tc % 2 + 1) * 512]
            P.mm(po, [(s_lhsT[:, wb + kc * 128:wb + (kc + 1) * 128], xt(kc, tc))
                      for kc in range(kc_n)], tag="i8mm")
            P.act(s_o[:], po, fn, scale=s_oc[:])
            P.dma("gpsimd", out[mt][:, tc * 512:(tc + 1) * 512], s_o[:])
    P.final_waits("gpsimd")
    return P.finalize()


def build_rmsq(name, D, T=2048, eps_norm=1e-8, eps_q=1e-5):
    P = Prog(name)
    x = P.din("x", [T, D], F32)
    nw = P.din("nw", [2, D], F32)
    xq = P.dout("xq", [T, D], BF16)
    srow = P.dout("srow", [1, T], F32)
    s_nw = P.sb("s_nw", [2, D], F32)
    P.dma("sync", s_nw[:], nw)
    s_nwr = P.sb("s_nwr", [128, D], F32)
    p_b = P.ps("p_b", [128, 512])
    for j in range(D // 512):
        sl = slice(j * 512, (j + 1) * 512)
        P.mm(p_b[:], [(P.bcrow("r0"), s_nw[:, sl])], tag="bcnw")
        P.scopy(s_nwr[:, sl], p_b[:])
    s_x = P.sb("s_x", [128, 2 * D], F32)
    s_sq = P.sb("s_sq", [128, D], F32)
    s_c1 = P.sb("s_c1", [128, 1], F32)
    s_c2 = P.sb("s_c2", [128, 1], F32)
    s_xh = P.sb("s_xh", [128, D], F32)
    s_a = P.sb("s_a", [128, 1], F32)
    s_st = P.sb("s_st", [128, 1], F32)
    s_si = P.sb("s_si", [128, 1], F32)
    s_xi = P.sb("s_xi", [128, 512], I32)
    s_xf = P.sb("s_xf", [128, 512], F32)
    s_q = P.sb("s_q", [128, D], BF16)
    for i in range(T // 128):
        xb = s_x[:, (i % 2) * D:(i % 2 + 1) * D]
        P.dma("sync", xb, x[i * 128:(i + 1) * 128, :])
        P.act(s_sq[:], xb, AF.Square)
        P.red("vector", s_c1[:], s_sq[:], ALU.add)
        P.ts("vector", s_c1[:], s_c1[:], 1.0 / D, eps_norm, op0=ALU.mult, op1=ALU.add)
        P.act(s_c1[:], s_c1[:], AF.Sqrt)
        P.op("vector", lambda e: e.reciprocal(s_c2[:], s_c1[:]), reads=[s_c1[:]], writes=[s_c2[:]])
        P.ts("vector", s_xh[:], xb, s_c2[:], None, op0=ALU.mult)
        P.tt("vector", s_xh[:], s_xh[:], s_nwr[:], ALU.mult)
        P.red("vector", s_a[:], s_xh[:], ALU.max, absval=True)
        P.ts("vector", s_a[:], s_a[:], eps_q, None, op0=ALU.max)
        P.op("vector", lambda e: e.reciprocal(s_st[:], s_a[:]), reads=[s_a[:]], writes=[s_st[:]])
        P.ts("vector", s_st[:], s_st[:], 127.0, None, op0=ALU.mult)
        P.ts("vector", s_si[:], s_a[:], 1.0 / 127.0, None, op0=ALU.mult)
        for j in range(D // 512):
            sl = slice(j * 512, (j + 1) * 512)
            P.ts("vector", s_xf[:], s_xh[:, sl], s_st[:], None, op0=ALU.mult)
            P.copy("vector", s_xi[:], s_xf[:])
            P.copy("vector", s_xf[:], s_xi[:])
            P.ts("vector", s_q[:, sl], s_xf[:], 127.0, -128.0, op0=ALU.min, op1=ALU.max)
        P.dma("gpsimd", xq[i * 128:(i + 1) * 128, :], s_q[:])
        P.dma("gpsimd", srow[0:1, i * 128:(i + 1) * 128], s_si[:])
    P.final_waits("gpsimd")
    return P.finalize()


def build_lin_tern(name, K, M, T=2048, post="h3"):
    P = Prog(name)
    kc_n, mt_n, ntc = K // 128, M // 128, T // 512
    x = P.din("x", [kc_n, 128, T], BF16)
    wrm = P.din("wrm", [mt_n, 128, K], F32)
    srow_i = P.din("srow", [1, T], F32)
    out = P.dout("out", [mt_n, 128, T], F32)
    if post == "h3":
        asab = P.din("asab", [2, 2], F32)
        s_asab = P.sb("s_asab", [2, 2], F32)
        P.dma("sync", s_asab[:], asab)

    s_x = P.sb("s_x", [128, kc_n * T], BF16)
    for kc in range(kc_n):
        P.dma("sync", s_x[:, kc * T:(kc + 1) * T], x[kc])

    s_wrm = P.sb("s_wrm", [128, K], F32)
    s_abs = P.sb("s_abs", [128, 512], F32)
    s_rs = P.sb("s_rs", [128, 1], F32)
    s_rsa = P.sb("s_rsa", [128, 1], F32)
    s_sum = P.sb("s_sum", [128, 1], F32)
    s_g = P.sb("s_g", [2, 2], F32)
    P.memset("vector", s_sum[:], 0.0)
    P.memset("vector", s_g[:], 0.0)
    for mt in range(mt_n):
        P.dma("sync", s_wrm[:], wrm[mt])
        for j in range(K // 512):
            sl = slice(j * 512, (j + 1) * 512)
            P.ts("vector", s_abs[:], s_wrm[:, sl], -1.0, None, op0=ALU.mult)
            P.tt("vector", s_abs[:], s_abs[:], s_wrm[:, sl], ALU.max)
            P.red("vector", s_rs[:], s_abs[:], ALU.add)
            P.tt("vector", s_sum[:], s_sum[:], s_rs[:], ALU.add)
    P.red("gpsimd", s_g[0:1, 0:1], s_sum[:], ALU.add, axis=AX.C)
    P.ts("vector", s_g[0:1, 0:1], s_g[0:1, 0:1], 1.0 / (M * K), None, op0=ALU.mult)
    P.ts("vector", s_g[0:1, 0:1], s_g[0:1, 0:1], 1e-5, None, op0=ALU.max)
    s_ws = P.sb("s_ws", [2, 2], F32)
    P.memset("vector", s_ws[:], 0.0)
    P.op("vector", lambda e: e.reciprocal(s_ws[0:1, 0:1], s_g[0:1, 0:1]),
         reads=[s_g[:]], writes=[s_ws[:]])
    p_bc = P.ps("p_bc", [128, 2])
    s_wsc = P.sb("s_wsc", [128, 1], F32)
    s_invc = P.sb("s_invc", [128, 1], F32)
    P.mm(p_bc[:], [(P.bcrow("r0"), s_ws[:, 0:2])], tag="bcws")
    P.scopy(s_wsc[:], p_bc[:, 0:1])
    P.mm(p_bc[:], [(P.bcrow("r0"), s_g[:, 0:2])], tag="bcinv")
    P.scopy(s_invc[:], p_bc[:, 0:1])
    if post == "h3":
        s_sqas = P.sb("s_sqas", [2, 2], F32)
        P.act(s_sqas[:], s_asab[:], AF.Sqrt)
        s_cp = P.sb("s_cp", [128, 1], F32)
        s_abc = P.sb("s_abc", [128, 1], F32)
        P.mm(p_bc[:], [(P.bcrow("r0"), s_sqas[:, 0:2])], tag="bcsq")
        P.scopy(s_cp[:], p_bc[:, 0:1])
        P.tt("vector", s_cp[:], s_cp[:], s_invc[:], ALU.mult)
        P.mm(p_bc[:], [(P.bcrow("r0"), s_asab[:, 1:3] if False else s_asab[:, 0:2])], tag="bcab")
        s_ab2 = P.sb("s_ab2", [2, 2], F32)
        P.memset("vector", s_ab2[:], 0.0)
        P.copy("vector", s_ab2[0:1, 0:1], s_asab[0:1, 1:2])
        P.mm(p_bc[:], [(P.bcrow("r0"), s_ab2[:, 0:2])], tag="bcab2")
        P.scopy(s_abc[:], p_bc[:, 0:1])
    s_srow = P.sb("s_srow", [2, T], F32)
    P.memset("vector", s_srow[:], 0.0)
    P.dma("sync", s_srow[0:1, :], srow_i)
    if post == "h3":
        P.tt("vector", s_srow[0:1, :], s_srow[0:1, :], s_srow[0:1, :], ALU.mult)
    s_srr = P.sb("s_srr", [128, T], F32)
    p_sr = P.ps("p_sr", [128, 512])
    for tc in range(ntc):
        sl = slice(tc * 512, (tc + 1) * 512)
        P.mm(p_sr[:], [(P.bcrow("r0"), s_srow[:, sl])], tag="bcsr")
        P.scopy(s_srr[:, sl], p_sr[:])

    s_wqb = P.sb("s_wqb", [128, K], BF16)
    s_wi = P.sb("s_wi", [128, 512], I32)
    s_lhsT = P.sb("s_lhsT", [128, 2 * K], BF16)
    p_tr = P.ps("p_tr", [128, 128], BF16)
    p_o = P.ps("p_o", [128, 2 * 512])
    s_r = P.sb("s_r", [128, 512], F32)
    s_o = P.sb("s_o", [128, 512], F32)
    for mt in range(mt_n):
        wb = (mt % 2) * K
        P.dma("sync", s_wrm[:], wrm[mt])
        for j in range(K // 512):
            sl = slice(j * 512, (j + 1) * 512)
            P.ts("vector", s_abs[:], s_wrm[:, sl], s_wsc[:], None, op0=ALU.mult)
            P.copy("vector", s_wi[:], s_abs[:])
            P.copy("vector", s_abs[:], s_wi[:])
            P.ts("vector", s_wqb[:, sl], s_abs[:], 1.0, -1.0, op0=ALU.min, op1=ALU.max)
        for kc in range(kc_n):
            P.transpose(p_tr[:], s_wqb[:, kc * 128:(kc + 1) * 128], P.iden())
            P.scopy(s_lhsT[:, wb + kc * 128:wb + (kc + 1) * 128], p_tr[:])
        for tc in range(ntc):
            sl = slice(tc * 512, (tc + 1) * 512)
            po = p_o[:, (tc % 2) * 512:(tc % 2 + 1) * 512]
            P.mm(po, [(s_lhsT[:, wb + kc * 128:wb + (kc + 1) * 128],
                       s_x[:, kc * T + tc * 512: kc * T + (tc + 1) * 512])
                      for kc in range(kc_n)], tag="tern")
            if post == "h3":
                P.act(s_r[:], po, AF.Relu)
                P.act(s_r[:], s_r[:], AF.Square, scale=s_cp[:])
                P.tt("vector", s_r[:], s_r[:], s_srr[:, sl], ALU.mult)
                P.ts("vector", s_o[:], s_r[:], s_abc[:], None, op0=ALU.add)
            else:
                P.act(s_r[:], po, AF.Identity, scale=s_invc[:])
                P.tt("vector", s_r[:], s_r[:], s_srr[:, sl], ALU.mult)
                P.act(s_o[:], s_r[:], AF.Silu)
            P.dma("gpsimd", out[mt][:, sl], s_o[:])
    P.final_waits("gpsimd")
    return P.finalize()


def build_elt_dropg(T=2048, FT=8, name="dropg"):
    P = Prog(name)
    y = P.din("y", [FT, 128, T], BF16)
    m = P.din("m", [FT, 128, T], BF16)
    g = P.dout("g", [FT, 128, T], F32)
    s_y = P.sb("s_y", [128, T], BF16)
    s_m = P.sb("s_m", [128, T], BF16)
    s_t = P.sb("s_t", [128, T], F32)
    s_g = P.sb("s_g", [128, T], F32)
    for ft in range(FT):
        P.dma("sync", s_y[:], y[ft])
        P.dma("sync", s_m[:], m[ft])
        P.tt("vector", s_t[:], s_y[:], s_m[:], ALU.mult)
        P.act(s_g[:], s_t[:], AF.Silu)
        P.dma("gpsimd", g[ft], s_g[:])
    P.final_waits("gpsimd")
    return P.finalize()


def build_elt_ysz(T=2048, FT=16, name="ysz"):
    P = Prog(name)
    y = P.din("y", [FT, 128, T], F32)
    z = P.din("z", [FT, 128, T], BF16)
    o = P.dout("o", [FT, 128, T], BF16)
    s_y = P.sb("s_y", [128, T], F32)
    s_z = P.sb("s_z", [128, T], BF16)
    s_sz = P.sb("s_sz", [128, T], F32)
    s_o = P.sb("s_o", [128, T], BF16)
    for ft in range(FT):
        P.dma("sync", s_y[:], y[ft])
        P.dma("sync", s_z[:], z[ft])
        P.act(s_sz[:], s_z[:], AF.Silu)
        P.tt("vector", s_o[:], s_y[:], s_sz[:], ALU.mult)
        P.dma("gpsimd", o[ft], s_o[:])
    P.final_waits("gpsimd")
    return P.finalize()


def build_elt_glu_skip(T=2048, FT=8, name="glusk"):
    P = Prog(name)
    gg = P.din("gg", [2 * FT, 128, T], F32)
    sk = P.din("sk", [FT, 128, T], F32)
    o = P.dout("o", [FT, 128, T], F32)
    s_1 = P.sb("s_1", [128, T], F32)
    s_2 = P.sb("s_2", [128, T], F32)
    s_k = P.sb("s_k", [128, T], F32)
    s_o = P.sb("s_o", [128, T], F32)
    for ft in range(FT):
        P.dma("sync", s_1[:], gg[ft])
        P.dma("sync", s_2[:], gg[FT + ft])
        P.dma("sync", s_k[:], sk[ft])
        P.act(s_2[:], s_2[:], AF.Sigmoid)
        P.tt("vector", s_o[:], s_1[:], s_2[:], ALU.mult)
        P.tt("vector", s_o[:], s_o[:], s_k[:], ALU.add)
        P.dma("gpsimd", o[ft], s_o[:])
    P.final_waits("gpsimd")
    return P.finalize()


def build_elt_combine(T=2048, FT=8, name="combine"):
    P = Prog(name)
    x = P.din("x", [FT, 128, T], F32)
    s = P.din("s", [FT, 128, T], F32)
    g = P.din("g", [FT, 128, T], F32)
    o = P.dout("o", [FT, 128, T], F32)
    s_x = P.sb("s_x", [128, T], F32)
    s_s = P.sb("s_s", [128, T], F32)
    s_g = P.sb("s_g", [128, T], F32)
    s_o = P.sb("s_o", [128, T], F32)
    for ft in range(FT):
        P.dma("sync", s_x[:], x[ft])
        P.dma("sync", s_s[:], s[ft])
        P.dma("sync", s_g[:], g[ft])
        P.tt("vector", s_o[:], s_s[:], s_g[:], ALU.mult)
        P.tt("vector", s_o[:], s_o[:], s_x[:], ALU.add)
        P.dma("gpsimd", o[ft], s_o[:])
    P.final_waits("gpsimd")
    return P.finalize()


def build_elt_h2(T=2048, FT=22, name="h2"):
    P = Prog(name)
    g = P.din("g", [FT, 128, T], F32)
    u = P.din("u", [FT, 128, T], F32)
    h = P.dout("h", [FT, 128, T], F32)
    am = P.dout("am", [1, 1], F32)
    s_g = P.sb("s_g", [128, T], F32)
    s_u = P.sb("s_u", [128, T], F32)
    s_h = P.sb("s_h", [128, T], F32)
    s_m = P.sb("s_m", [128, 1], F32)
    s_acc = P.sb("s_acc", [128, 1], F32)
    s_am = P.sb("s_am", [1, 2], F32)
    P.memset("vector", s_acc[:], 0.0)
    for ft in range(FT):
        P.dma("sync", s_g[:], g[ft])
        P.dma("sync", s_u[:], u[ft])
        P.tt("vector", s_h[:], s_g[:], s_u[:], ALU.mult)
        P.red("vector", s_m[:], s_h[:], ALU.max, absval=True)
        P.tt("vector", s_acc[:], s_acc[:], s_m[:], ALU.max)
        P.dma("gpsimd", h[ft], s_h[:])
    P.red("gpsimd", s_am[0:1, 0:1], s_acc[:], ALU.max, axis=AX.C)
    P.dma("gpsimd", am, s_am[0:1, 0:1])
    P.final_waits("gpsimd")
    return P.finalize()


def build_sig1(name="sig1"):
    P = Prog(name)
    og = P.din("og", [2, 2], F32)
    o = P.dout("o", [1, 1], F32)
    s = P.sb("s_og", [2, 2], F32)
    P.dma("sync", s[:], og)
    P.act(s[:], s[:], AF.Sigmoid)
    P.dma("gpsimd", o, s[0:1, 0:1])
    P.final_waits("gpsimd")
    return P.finalize()


"""Unfused multi-launch kernel for mk_recurrent_d4 (v1).

Host code between launches does ONLY layout: slicing, transposes, retiling,
concatenation, padding with constants, and broadcasting device-computed
scalars. All arithmetic on values happens on the NeuronCores.
"""
from concourse.bass_utils import run_bass_kernel_spmd

NPBF16 = ml_dtypes.bfloat16
NC = 8
T = 2048
S = 8192
D = 1024

_P = {}


def prog(name, builder, *a, **k):
    if name not in _P:
        _P[name] = builder(*a, **k)
    return _P[name]


def launch(nc, in_maps):
    res = run_bass_kernel_spmd(nc, in_maps, core_ids=list(range(NC)))
    return res.results


def tfm(a):
    return tile_fm(np.ascontiguousarray(a))


def ufm(a):
    return untile_fm(a)


def tlhsT(w):
    return tile_lhsT(np.ascontiguousarray(w))


def colM(v):
    v = np.asarray(v, np.float32)
    return np.ascontiguousarray(v.reshape(-1, 128, 1))


def sel16():
    if "sel16" not in _CONST_CACHE:
        a = np.zeros((16, 2048), np.float32)
        for n in range(16):
            a[n, n * 128:(n + 1) * 128] = 1.0
        _CONST_CACHE["sel16"] = a
    return _CONST_CACHE["sel16"]


def pad2(v00, v01=0.0):
    a = np.zeros((2, 2), np.float32)
    a[0, 0] = v00
    a[0, 1] = v01
    return a


def cores_of_b(b):
    return range(b * 4, (b + 1) * 4)


_CONST_CACHE = {}


def rope_tables():
    if "rope" not in _CONST_CACHE:
        RD = 64
        inv = (1.0 / 10000.0 ** (np.arange(0, RD, 2, dtype=np.float32) / RD)).astype(np.float32)
        pos = np.arange(S, dtype=np.float32)
        fr = pos[:, None] * inv[None, :]
        emb = np.concatenate([fr, fr], axis=-1)
        _CONST_CACHE["rope"] = (np.cos(emb).astype(np.float32).T.copy(),
                                np.sin(emb).astype(np.float32).T.copy())
    return _CONST_CACHE["rope"]


def dropout_mask():
    if "mask" not in _CONST_CACHE:
        import jax
        with jax.default_device(jax.devices("cpu")[0]):
            keep = np.asarray(jax.random.bernoulli(jax.random.key(42), 0.8, (2, S, D)))
        _CONST_CACHE["mask"] = np.where(keep, np.float32(1.25), np.float32(0.0)).astype(NPBF16)
    return _CONST_CACHE["mask"]


def causal_masks():
    if "cm" not in _CONST_CACHE:
        out = []
        for c in range(NC):
            q0 = (c % 4) * T
            mq = np.zeros((T // 128, S // 512, 128, 512), np.float32)
            for qt in range(T // 128):
                qpos = q0 + qt * 128 + np.arange(128)
                for kb in range(S // 512):
                    kpos = kb * 512 + np.arange(512)
                    mq[qt, kb][kpos[None, :] > qpos[:, None]] = -1e30
            mt = np.zeros((T // 512, S // 128, 128, 512), np.float32)
            for qc in range(T // 512):
                qpos = q0 + qc * 512 + np.arange(512)
                for kb in range(S // 128):
                    kpos = kb * 128 + np.arange(128)
                    mt[qc, kb][kpos[:, None] > qpos[None, :]] = -1e30
            out.append((mq.astype(NPBF16), mt.astype(NPBF16)))
        _CONST_CACHE["cm"] = out
    return _CONST_CACHE["cm"]


def shard_tm(xfull):
    return [np.ascontiguousarray(xfull[c // 4, (c % 4) * T:(c % 4 + 1) * T]) for c in range(NC)]


def ln_all(xs, gamma_beta=None):
    if gamma_beta is None:
        nc = prog("ln", build_ln)
        r = launch(nc, [dict(x=xs[c]) for c in range(NC)])
    else:
        nc = prog("ln_gb", build_ln, gamma_beta=True)
        gb = np.zeros((2, D), np.float32)
        gb[0] = gamma_beta[0]
        gb[1] = gamma_beta[1]
        r = launch(nc, [dict(x=xs[c], gam=gb) for c in range(NC)])
    return ([r[c]["o"] for c in range(NC)], [r[c]["ob"].view(NPBF16) for c in range(NC)])


def lin_all(pname, xcs, w_t, K, M, **kw):
    nc = prog(pname, build_lin, pname, K, M, **{k: v for k, v in kw.items()
              if k not in ("b", "scol", "srow")})
    ims = []
    for c in range(NC):
        im = dict(x=xcs[c], w=w_t)
        for key in ("b", "scol"):
            if key in kw:
                im[key] = kw[key]
        if "srow" in kw:
            im["srow"] = kw["srow"][c] if isinstance(kw["srow"], list) else kw["srow"]
        ims.append(im)
    r = launch(nc, ims)
    return [r[c]["out"] for c in range(NC)]


def gather_fm_full(outs, b):
    return np.concatenate([ufm(outs[c]) for c in cores_of_b(b)], axis=1)


def layer0(xs, inp):
    w_kqv = np.asarray(inp["L0_attn_w_kqv"]).reshape(16, 64, 3, D)
    Wk = w_kqv[:, :, 0, :].reshape(D, D)
    Wq = w_kqv[:, :, 1, :].reshape(D, D)
    Wv = w_kqv[:, :, 2, :].reshape(D, D)
    xn, xnb = ln_all(xs)
    xnb_fm = [tfm(xnb[c].T) for c in range(NC)]
    wqkv = np.concatenate([Wq.T, Wk.T, Wv.T], axis=1)
    qkv = lin_all("lin_qkv0", xnb_fm, tlhsT(wqkv), D, 3072, x_in="bf16", out_dt="bf16")
    q = [np.ascontiguousarray(qkv[c][0:8]) for c in range(NC)]
    k = [np.ascontiguousarray(qkv[c][8:16]) for c in range(NC)]
    v = [np.ascontiguousarray(qkv[c][16:24]) for c in range(NC)]
    kf, va = [], []
    ones = np.ones((S, 1), NPBF16)
    for b in range(2):
        kfull = gather_fm_full([x.view(NPBF16) for x in k], b)
        kf.append(np.ascontiguousarray(kfull.reshape(8, 128, S)))
        vtm = gather_fm_full([x.view(NPBF16) for x in v], b).T
        cols = []
        for h in range(16):
            cols.append(vtm[:, h * 64:(h + 1) * 64])
            cols.append(ones)
            cols.append(ones)
        vaug = np.concatenate(cols, axis=1)
        va.append(np.ascontiguousarray(vaug.reshape(64, 128, 16 * 66)))
    masks = causal_masks()
    att = prog("att0", build_att, "att0", 16, 64, 16, causal=True)
    r = launch(att, [dict(q=q[c].view(NPBF16), kf=kf[c // 4], va=va[c // 4],
                          maskq=masks[c][0], maskt=masks[c][1],
                          c_iden=np.eye(128, dtype=NPBF16), c_sel16=sel16())
                     for c in range(NC)])
    ctx = [r[c]["out"].view(NPBF16) for c in range(NC)]
    s = lin_all("lin_kkb", ctx, tlhsT(np.asarray(inp["L0_attn_w_proj"]).T.copy()), D, D,
                x_in="bf16", out_dt="f32", bias=True, b=colM(inp["L0_attn_b_proj"]))
    rwT = np.zeros((D, 16), np.float32)
    rwT[:, 0:8] = np.asarray(inp["L0_ffn_router_w"]).T
    rr = launch(prog("router", build_router),
                [dict(x=xnb_fm[c], rw=tfm(rwT)) for c in range(NC)])
    ge_rows = []
    for c in range(NC):
        g = np.zeros((16, T), np.float32)
        g[0:8] = rr[c]["ge"].T
        ge_rows.append(g)
    wg = np.stack([tlhsT(np.asarray(inp["L0_ffn_w_gate"])[e]) for e in range(8)])
    wu = np.stack([tlhsT(np.asarray(inp["L0_ffn_w_up"])[e]) for e in range(8)])
    wd = np.stack([tlhsT(np.asarray(inp["L0_ffn_w_down"])[e]) for e in range(8)])
    rm = launch(prog("moe", build_moe),
                [dict(x=xnb_fm[c], ge=ge_rows[c], wg=wg, wu=wu, wd=wd, c_sel16=sel16())
                 for c in range(NC)])
    g = [rm[c]["g"] for c in range(NC)]
    x_fm = [tfm(xs[c].T) for c in range(NC)]
    rc = launch(prog("combine", build_elt_combine),
                [dict(x=x_fm[c], s=s[c], g=g[c]) for c in range(NC)])
    return [np.ascontiguousarray(ufm(rc[c]["o"]).T) for c in range(NC)]


def layer1(xs, inp):
    xn, xnb = ln_all(xs)
    xnb_fm = [tfm(xnb[c].T) for c in range(NC)]
    wq1 = np.concatenate([np.asarray(inp["L1_attn_Wq"]), np.asarray(inp["L1_attn_Wk"]),
                          np.asarray(inp["L1_attn_Wv"])], axis=1)
    qkv1 = lin_all("lin_qkv1", xnb_fm, tlhsT(wq1), D, 5120, x_in="bf16", out_dt="bf16")
    qg = [np.ascontiguousarray(qkv1[c][0:32]) for c in range(NC)]
    kv_k = [np.ascontiguousarray(qkv1[c][32:36]) for c in range(NC)]
    kv_v = [np.ascontiguousarray(qkv1[c][36:40]) for c in range(NC)]
    cosT, sinT = rope_tables()
    ropeq = prog("ropeq", build_rope, 8, 256)
    ropek = prog("ropek", build_rope, 2, 256)
    csl = [np.ascontiguousarray(cosT[:, (c % 4) * T:(c % 4 + 1) * T]) for c in range(NC)]
    ssl = [np.ascontiguousarray(sinT[:, (c % 4) * T:(c % 4 + 1) * T]) for c in range(NC)]
    rq = launch(ropeq, [dict(q=qg[c].view(NPBF16)[0:16], cos=csl[c], sin=ssl[c]) for c in range(NC)])
    rk = launch(ropek, [dict(q=kv_k[c].view(NPBF16), cos=csl[c], sin=ssl[c]) for c in range(NC)])
    qrope, krope = [], []
    for c in range(NC):
        qf = ufm(qg[c].view(NPBF16)[0:16]).copy()
        pq = rq[c]["o"].view(NPBF16)
        for h in range(8):
            qf[h * 256:h * 256 + 64] = pq[h]
        qrope.append(np.ascontiguousarray(qf.reshape(16, 128, T)))
        kfull = ufm(kv_k[c].view(NPBF16)).copy()
        pk = rk[c]["o"].view(NPBF16)
        for h in range(2):
            kfull[h * 256:h * 256 + 64] = pk[h]
        krope.append(kfull)
    kf, va = [], []
    ones = np.ones((S, 1), NPBF16)
    for b in range(2):
        kfull = np.concatenate([krope[c] for c in cores_of_b(b)], axis=1)
        kf.append(np.ascontiguousarray(kfull.reshape(4, 128, S)))
        vtm = gather_fm_full([x.view(NPBF16) for x in kv_v], b).T
        cols = []
        for kv in range(2):
            cols.append(vtm[:, kv * 256:(kv + 1) * 256])
            cols.append(ones)
            cols.append(ones)
        vaug = np.concatenate(cols, axis=1)
        va.append(np.ascontiguousarray(vaug.reshape(64, 128, 2 * 258)))
    att = prog("att1", build_att, "att1", 8, 256, 2, causal=False, gate=True)
    r = launch(att, [dict(q=qrope[c], kf=kf[c // 4], va=va[c // 4],
                          gt=np.ascontiguousarray(qg[c].view(NPBF16)[16:32]),
                          c_iden=np.eye(128, dtype=NPBF16), c_sel16=sel16())
                     for c in range(NC)])
    ctx = [r[c]["out"].view(NPBF16) for c in range(NC)]
    s = lin_all("lin_2k1k", ctx, tlhsT(np.asarray(inp["L1_attn_Wo"])), 2048, D,
                x_in="bf16", out_dt="f32")
    h = lin_all("lin_f8a", [tfm(xn[c].T) for c in range(NC)],
                tlhsT(np.asarray(inp["L1_ffn_W1"]).T.copy()), D, 4096,
                x_in="f32", cast="f8", epi="relu", out_dt="bf16",
                bias=True, b=colM(inp["L1_ffn_b1"]))
    y2 = lin_all("lin_f8b", [x.view(NPBF16) for x in h],
                 tlhsT(np.asarray(inp["L1_ffn_W2"]).T.copy()), 4096, D,
                 x_in="bf16", cast="f8", epi="copy", out_dt="bf16",
                 bias=True, b=colM(inp["L1_ffn_b2"]))
    dm = dropout_mask()
    dms = [tfm(dm[c // 4, (c % 4) * T:(c % 4 + 1) * T].T) for c in range(NC)]
    rg = launch(prog("dropg", build_elt_dropg),
                [dict(y=y2[c].view(NPBF16), m=dms[c]) for c in range(NC)])
    g = [rg[c]["g"] for c in range(NC)]
    x_fm = [tfm(xs[c].T) for c in range(NC)]
    rc = launch(prog("combine", build_elt_combine),
                [dict(x=x_fm[c], s=s[c], g=g[c]) for c in range(NC)])
    return [np.ascontiguousarray(ufm(rc[c]["o"]).T) for c in range(NC)]


def layer2(xs, inp):
    DI = 2048
    xn, xnb = ln_all(xs)
    xnb_fm = [tfm(xnb[c].T) for c in range(NC)]
    xn_fm = [tfm(xn[c].T) for c in range(NC)]
    ra = launch(prog("amax", build_amax), [dict(x=xn[c]) for c in range(NC)])
    gparts = np.zeros((2, 8), np.float32)
    for c in range(NC):
        gparts[0, c] = ra[c]["o"][0, 0]
    xz = lin_all("lin_1k4k", xnb_fm, tlhsT(np.asarray(inp["L2_attn_W_in"]).T.copy()),
                 D, 4096, x_in="bf16", out_dt="bf16")
    xssm = [np.ascontiguousarray(xz[c].view(NPBF16)[0:16]) for c in range(NC)]
    zb = [np.ascontiguousarray(xz[c].view(NPBF16)[16:32]) for c in range(NC)]
    dt = lin_all("lin_sp2k", xssm, tlhsT(np.asarray(inp["L2_attn_W_dt"]).T.copy()),
                 DI, DI, x_in="bf16", epi="softplus", out_dt="bf16",
                 bias=True, b=colM(inp["L2_attn_b_dt"]))
    wbc = np.zeros((DI, 128), np.float32)
    wbc[:, 0:16] = np.asarray(inp["L2_attn_W_B"]).T
    wbc[:, 16:32] = np.asarray(inp["L2_attn_W_C"]).T
    bc = lin_all("lin_bc2k", xssm, tlhsT(wbc), DI, 128, x_in="bf16", out_dt="f32")
    scan = prog("scan", build_scan, "scan")
    ims = []
    for c in range(NC):
        b = c // 4
        dsl = slice((c % 4) * 512, (c % 4 + 1) * 512)
        uf = np.concatenate([ufm(xssm[cc]) for cc in cores_of_b(b)], axis=1)[dsl]
        dtf = np.concatenate([ufm(dt[cc].view(NPBF16)) for cc in cores_of_b(b)], axis=1)[dsl]
        bcf = np.concatenate([ufm(bc[cc]) for cc in cores_of_b(b)], axis=1)
        brow = np.ascontiguousarray(bcf[0:16]); crow = np.ascontiguousarray(bcf[16:32])
        ims.append(dict(u=np.ascontiguousarray(uf.reshape(4, 128, S)),
                        dt=np.ascontiguousarray(dtf.reshape(4, 128, S)),
                        Brow=brow, Crow=crow,
                        Alog=np.ascontiguousarray(np.asarray(inp["L2_attn_A_log"])[dsl].reshape(4, 128, 16)),
                        Dcol=np.ascontiguousarray(np.asarray(inp["L2_attn_D"])[dsl].reshape(4, 128, 1)),
                        c_sel16=sel16()))
    rs = launch(scan, ims)
    y_fm = []
    for c in range(NC):
        b = c // 4
        yfull = np.concatenate([rs[cc]["y"].reshape(512, S) for cc in cores_of_b(b)], axis=0)
        y_fm.append(np.ascontiguousarray(yfull[:, (c % 4) * T:(c % 4 + 1) * T].reshape(16, 128, T)))
    rz = launch(prog("ysz", build_elt_ysz),
                [dict(y=y_fm[c], z=zb[c]) for c in range(NC)])
    ym = [rz[c]["o"].view(NPBF16) for c in range(NC)]
    rsg = launch(prog("sig1", build_sig1),
                 [dict(og=pad2(float(np.asarray(inp["L2_attn_output_gate"]).ravel()[0]))) for _ in range(NC)])
    sgrow = [np.broadcast_to(rsg[c]["o"].reshape(1, 1), (1, T)).astype(np.float32) for c in range(NC)]
    s = lin_all("lin_wo2k", ym, tlhsT(np.asarray(inp["L2_attn_W_out"]).T.copy()), DI, D,
                x_in="bf16", out_dt="f32", scale_row=True, srow=sgrow)
    wgu = np.concatenate([np.asarray(inp["L2_ffn_w_gate"]), np.asarray(inp["L2_ffn_w_up"])], axis=0)
    r1 = launch(prog("i8gu", build_lin_i8, "i8gu", D, 5632, silu_until=22),
                [dict(x=xn_fm[c], wrm=np.ascontiguousarray(wgu.reshape(44, 128, D)),
                      gparts=gparts, clip=pad2(float(np.asarray(inp["L2_ffn_clip_x"]).ravel()[0])),
                      c_iden=np.eye(128, dtype=NPBF16)) for c in range(NC)])
    gu = [r1[c]["out"] for c in range(NC)]
    rh = launch(prog("h2", build_elt_h2),
                [dict(g=np.ascontiguousarray(gu[c][0:22]), u=np.ascontiguousarray(gu[c][22:44]))
                 for c in range(NC)])
    hparts = np.zeros((2, 8), np.float32)
    for c in range(NC):
        hparts[0, c] = rh[c]["am"][0, 0]
    h2 = [rh[c]["h"] for c in range(NC)]
    g2 = launch(prog("i8dn", build_lin_i8, "i8dn", 2816, D, epi_all="silu"),
                [dict(x=h2[c], wrm=np.ascontiguousarray(np.asarray(inp["L2_ffn_w_down"]).reshape(8, 128, 2816)),
                      gparts=hparts, clip=pad2(float(np.asarray(inp["L2_ffn_clip_down"]).ravel()[0])),
                      c_iden=np.eye(128, dtype=NPBF16)) for c in range(NC)])
    g = [g2[c]["out"] for c in range(NC)]
    x_fm = [tfm(xs[c].T) for c in range(NC)]
    rc = launch(prog("combine", build_elt_combine),
                [dict(x=x_fm[c], s=s[c], g=g[c]) for c in range(NC)])
    return [np.ascontiguousarray(ufm(rc[c]["o"]).T) for c in range(NC)]


def layer3(xs, inp):
    DI = 2048
    xn, xnb = ln_all(xs)
    h3, h3b = ln_all(xn, gamma_beta=(np.asarray(inp["L3_attn_ln_gamma"]),
                                     np.asarray(inp["L3_attn_ln_beta"])))
    h3_fm = [tfm(h3b[c].T) for c in range(NC)]
    xz = lin_all("lin_1k4k", h3_fm, tlhsT(np.asarray(inp["L3_attn_in_proj_w"])),
                 D, 4096, x_in="bf16", out_dt="bf16")
    xi = [np.ascontiguousarray(xz[c].view(NPBF16)[0:16]) for c in range(NC)]
    z3 = [np.ascontiguousarray(xz[c].view(NPBF16)[16:32]) for c in range(NC)]
    conv = prog("conv3", build_conv3)
    ims = []
    for c in range(NC):
        b = c // 4
        dsl = slice((c % 4) * 512, (c % 4 + 1) * 512)
        xif = np.concatenate([ufm(xi[cc]) for cc in cores_of_b(b)], axis=1)[dsl]
        ims.append(dict(xi=np.ascontiguousarray(xif.reshape(4, 128, S)),
                        cw=np.ascontiguousarray(np.asarray(inp["L3_attn_conv_w"])[dsl].reshape(4, 128, 4)),
                        cb=np.ascontiguousarray(np.asarray(inp["L3_attn_conv_b"])[dsl].reshape(4, 128, 1))))
    rcv = launch(conv, ims)
    xi2_fm = []
    for c in range(NC):
        b = c // 4
        xif = np.concatenate([rcv[cc]["xo"].view(NPBF16).reshape(512, S) for cc in cores_of_b(b)], axis=0)
        xi2_fm.append(np.ascontiguousarray(xif[:, (c % 4) * T:(c % 4 + 1) * T].reshape(16, 128, T)))
    wxp = np.zeros((DI, 128), np.float32)
    wxp[:, 0:96] = np.asarray(inp["L3_attn_x_proj_w"])
    xdb = lin_all("lin_bc2k", xi2_fm, tlhsT(wxp), DI, 128, x_in="bf16", out_dt="f32")
    wdt = np.zeros((128, DI), np.float32)
    wdt[0:64] = np.asarray(inp["L3_attn_dt_proj_w"])
    dt3 = lin_all("lin_spdt", xdb, tlhsT(wdt), 128, DI, x_in="f32", epi="softplus",
                  out_dt="bf16", bias=True, b=colM(inp["L3_attn_dt_proj_b"]))
    scan = prog("scan", build_scan, "scan")
    ims = []
    for c in range(NC):
        b = c // 4
        dsl = slice((c % 4) * 512, (c % 4 + 1) * 512)
        dtf = np.concatenate([ufm(dt3[cc].view(NPBF16)) for cc in cores_of_b(b)], axis=1)[dsl]
        xdbf = np.concatenate([ufm(xdb[cc]) for cc in cores_of_b(b)], axis=1)
        brow = np.ascontiguousarray(xdbf[64:80]); crow = np.ascontiguousarray(xdbf[80:96])
        xif = np.concatenate([rcv[cc]["xo"].view(NPBF16).reshape(512, S) for cc in cores_of_b(b)], axis=0)[dsl]
        ims.append(dict(u=np.ascontiguousarray(xif.reshape(4, 128, S)),
                        dt=np.ascontiguousarray(dtf.reshape(4, 128, S)),
                        Brow=brow, Crow=crow,
                        Alog=np.ascontiguousarray(np.asarray(inp["L3_attn_A_log"])[dsl].reshape(4, 128, 16)),
                        Dcol=np.ascontiguousarray(np.asarray(inp["L3_attn_D"])[dsl].reshape(4, 128, 1)),
                        c_sel16=sel16()))
    rs = launch(scan, ims)
    y_fm = []
    for c in range(NC):
        b = c // 4
        yfull = np.concatenate([rs[cc]["y"].reshape(512, S) for cc in cores_of_b(b)], axis=0)
        y_fm.append(np.ascontiguousarray(yfull[:, (c % 4) * T:(c % 4 + 1) * T].reshape(16, 128, T)))
    rz = launch(prog("ysz", build_elt_ysz),
                [dict(y=y_fm[c], z=z3[c]) for c in range(NC)])
    ym = [rz[c]["o"].view(NPBF16) for c in range(NC)]
    hg = lin_all("lin_gel2k", ym, tlhsT(np.asarray(inp["L3_attn_out_proj_w"])), DI, D,
                 x_in="bf16", epi="gelu", out_dt="bf16")
    gg = lin_all("lin_glu", [x.view(NPBF16) for x in hg], tlhsT(np.asarray(inp["L3_attn_glu_w"])),
                 D, DI, x_in="bf16", out_dt="f32", bias=True, b=colM(inp["L3_attn_glu_b"]))
    rgs = launch(prog("glusk", build_elt_glu_skip),
                 [dict(gg=gg[c], sk=tfm(xn[c].T)) for c in range(NC)])
    s = [rgs[c]["o"] for c in range(NC)]
    nw1 = np.zeros((2, D), np.float32); nw1[0] = np.asarray(inp["L3_ffn_norm1_w"])
    rq1 = launch(prog("rmsq1", build_rmsq, "rmsq1", D),
                 [dict(x=xn[c], nw=nw1) for c in range(NC)])
    asab = pad2(float(np.asarray(inp["L3_ffn_act_scale"]).ravel()[0]),
                float(np.asarray(inp["L3_ffn_act_bias"]).ravel()[0]))
    r1 = launch(prog("tern1", build_lin_tern, "tern1", D, 4096, post="h3"),
                [dict(x=tfm(rq1[c]["xq"].view(NPBF16).T), wrm=np.ascontiguousarray(np.asarray(inp["L3_ffn_w1"]).reshape(32, 128, D)),
                      srow=rq1[c]["srow"], asab=asab, c_iden=np.eye(128, dtype=NPBF16))
                 for c in range(NC)])
    h3f = [np.ascontiguousarray(ufm(r1[c]["out"]).T) for c in range(NC)]
    nw2 = np.zeros((2, 4096), np.float32); nw2[0] = np.asarray(inp["L3_ffn_norm2_w"])
    rq2 = launch(prog("rmsq2", build_rmsq, "rmsq2", 4096),
                 [dict(x=h3f[c], nw=nw2) for c in range(NC)])
    r2 = launch(prog("tern2", build_lin_tern, "tern2", 4096, D, post="silu"),
                [dict(x=tfm(rq2[c]["xq"].view(NPBF16).T), wrm=np.ascontiguousarray(np.asarray(inp["L3_ffn_w2"]).reshape(8, 128, 4096)),
                      srow=rq2[c]["srow"], c_iden=np.eye(128, dtype=NPBF16))
                 for c in range(NC)])
    g = [r2[c]["out"] for c in range(NC)]
    x_fm = [tfm(xs[c].T) for c in range(NC)]
    rc = launch(prog("combine", build_elt_combine),
                [dict(x=x_fm[c], s=s[c], g=g[c]) for c in range(NC)])
    return [np.ascontiguousarray(ufm(rc[c]["o"]).T) for c in range(NC)]


def kernel(**inp):
    x = np.asarray(inp["L0_attn_x"], np.float32)
    xs = shard_tm(x)
    xs = layer0(xs, inp)
    xs = layer1(xs, inp)
    xs = layer2(xs, inp)
    xs = layer3(xs, inp)
    out = np.empty((2, S, D), np.float32)
    for c in range(NC):
        out[c // 4, (c % 4) * T:(c % 4 + 1) * T] = xs[c]
    return out
